# version 37
# baseline (speedup 1.0000x reference)
import numpy as np
from collections import defaultdict
from contextlib import ExitStack

import concourse.bass as bass
import concourse.mybir as mybir
from concourse.bass_utils import run_bass_kernel_spmd

F32 = mybir.dt.float32
BF16 = mybir.dt.bfloat16
FP16 = mybir.dt.float16
AF = mybir.ActivationFunctionType
ALU = mybir.AluOpType
AX = mybir.AxisListType

NCORES = 8
NSEQ = 4
S = 2048
NT = 16
H = 1024
HC = 8
NH = 16
HD = 64
INNER = 4096
IC = 32
ICB = 8
SGT = 2
NSG = (NSEQ * NT) // SGT
NTILE = NSEQ * NT
EPS = 1e-5
EXPOFF = -12.0
MAGIC = 1536.0


def qb_sched(qb):
    if qb == 0:
        return list(range(16)), ["QB0A", "C0x4", "C0x4", "C0x4"]
    if qb == 1:
        return [0, 1, 2, 0], ["QB1"]
    if qb <= 14:
        return [qb - 1, qb, qb + 1, 0], ["MID"]
    return [14, 15, 0, 13], ["QB15"]


def qslot(tg):
    return 5 if tg == 0 else tg % 5


A2_BATCH = {0: [1, 2], 1: [3, 4, 5, 6], 2: [7, 8, 9, 10],
            3: [11, 12, 13, 14, 15, 0]}


class ChainEng:

    PASS = ("wait_ge", "dma_start", "nop", "engine_nop")

    def __init__(self, wk, eng, ser, cnt0=0):
        self.wk = wk
        self.raw = eng
        self.ser = ser
        self.pend = None
        self.cnt = cnt0

    def __getattr__(self, name):
        real = getattr(self.raw, name)
        if name in ChainEng.PASS or not callable(real):
            return real

        def wrapped(*a, **k):
            self._settle_and_wait()
            instr = real(*a, **k)
            self.pend = instr
            return instr

        return wrapped

    def _settle_and_wait(self):
        wk, p = self.wk, self.pend
        if p is None:
            return
        rec = wk._edge_of.get(id(p))
        if rec is not None and rec[0] is p:
            edge = rec[1:]
        else:
            self.cnt += 1
            p.then_inc(wk.h.sems[self.ser], 1)
            edge = (self.ser, self.cnt)
            wk._edge_of[id(p)] = (p, self.ser, self.cnt)
        self.raw.wait_ge(wk.h.sems[edge[0]], edge[1])


class Walk:

    def __init__(self, hs, role, eng, counts=None, sercnt=0):
        self.h = hs
        self.R = role
        self._edge_of = {}
        if role == "v":
            eng = ChainEng(self, eng, "vser", sercnt)
        elif role == "s":
            eng = ChainEng(self, eng, "sser", sercnt)
        self.e = eng
        self.t = defaultdict(int, counts or {})
        self.caps = {}

    def tick(self, name):
        self.t[name] += 1
        return self.t[name]

    def inc(self, instr, sem, n=1):
        assert instr is not getattr(self, "_last", None), f"double inc {sem}"
        instr.then_inc(self.h.sems[sem], n)
        self._last = instr
        if n == 1:
            self._edge_of[id(instr)] = (instr, sem, self.t[sem])
        else:
            self._edge_of[id(instr)] = (instr, sem, -1)
        return instr

    def w(self, sem, val):
        if val > 0:
            self.e.wait_ge(self.h.sems[sem], val)

    def wdma(self, base, k):
        self.w(base + str((k - 1) % 2), ((k - 1) // 2 + 1) * 16)

    def idma(self, d, base, k):
        self.inc(d, base + str((k - 1) % 2), 16)

    def wdma4(self, base, k):
        self.w(base + str((k - 1) % 4), ((k - 1) // 4 + 1) * 16)

    def idma4(self, d, base, k):
        self.inc(d, base + str((k - 1) % 4), 16)

    def pfill(self, pool, kf):
        if kf > 1:
            self.w("e" + pool, kf - 1)

    def cap(self, key):
        self.caps[key] = (self.t["fPC"], self.t["fPD"])

    def ppsum(self, name):
        h = self.h
        if name == "PBa":
            return h.psum["PB"][:, 0:512]
        if name == "PBb":
            return h.psum["PB"][:, 512:1024]
        return h.psum[name][:]

    def wcap(self, key):
        c = self.caps.get(key)
        if c:
            self.w("fPC", c[0])
            self.w("fPD", c[1])

    def phase_consts(self):
        h = self.h
        if self.R == "g":
            g = self.e
            _k = [0]

            def gq(instr):
                _k[0] += 1
                instr.then_inc(h.sems["gser"], 1)
                g.wait_ge(h.sems["gser"], _k[0])

            for buf, neg in ((h.Ib, False), (h.nIb, True), (h.I16, False)):
                gq(g.memset(buf[:], -1.0 if neg else 1.0))
                gq(g.affine_select(buf[:], buf[:], [[-1, 128]], ALU.is_equal,
                                   0.0, channel_multiplier=1))
            gq(g.memset(h.m1536row[:], -MAGIC))
            gq(g.memset(h.ones16row[:], 1.0))
            gq(g.memset(h.onesf32[:], 1.0))
            gq(g.memset(h.onesf32row[:], 1.0))
            gq(g.memset(h.b1536[:], MAGIC))
            gq(g.memset(h.cm12[:], EXPOFF))
            gq(g.memset(h.cm1536[:], -MAGIC))
            gq(g.memset(h.c_eps[:], EPS))
            gq(g.memset(h.czero[:], 0.0))

            def sec(buf, j):
                return buf[:, 128 * j:128 * (j + 1)]

            def lt(bs):
                gq(g.memset(bs, 1.0))
                gq(g.affine_select(bs, bs, [[-1, 128]], ALU.is_ge, 0.0,
                                   channel_multiplier=1))

            def ut(bs):
                gq(g.memset(bs, 1.0))
                gq(g.affine_select(bs, bs, [[1, 128]], ALU.is_ge, 0.0,
                                   channel_multiplier=-1))

            gq(g.memset(sec(h.M_QB0A, 0), 1.0))
            ut(sec(h.M_QB0A, 1))
            gq(g.memset(sec(h.M_QB0A, 1)[:, 0:1], 1.0))
            for j in (2, 3):
                gq(g.memset(sec(h.M_QB0A, j), 0.0))
                gq(g.memset(sec(h.M_QB0A, j)[:, 0:1], 1.0))
            for j in range(4):
                gq(g.memset(sec(h.M_C0x4, j), 0.0))
                gq(g.memset(sec(h.M_C0x4, j)[:, 0:1], 1.0))
            lt(sec(h.M_QB1, 0))
            gq(g.memset(sec(h.M_QB1, 0)[0:1, :], 1.0))
            gq(g.memset(sec(h.M_QB1, 1), 1.0))
            ut(sec(h.M_QB1, 2))
            gq(g.memset(sec(h.M_QB1, 3), 0.0))
            lt(sec(h.M_MID, 0))
            gq(g.memset(sec(h.M_MID, 1), 1.0))
            ut(sec(h.M_MID, 2))
            gq(g.memset(sec(h.M_MID, 3), 0.0))
            gq(g.memset(sec(h.M_MID, 3)[0:1, :], 1.0))
            lt(sec(h.M_QB15, 0))
            gq(g.memset(sec(h.M_QB15, 1), 1.0))
            gq(g.memset(sec(h.M_QB15, 2), 0.0))
            gq(g.memset(sec(h.M_QB15, 2)[0:1, :], 1.0))
            gq(g.memset(sec(h.M_QB15, 3), 0.0))
            i = g.memset(h.Vt[:], 1.0)
            self.inc(i, "gc", 1)
        self.tick("gc")
        self.w("gc", 1)

    def phase_prep(self):
        h = self.h
        for mi, (wsrc, wdst) in enumerate(
                [(h.wq, h.wqT), (h.wk, h.wkT), (h.wv, h.wvT), (h.wo, h.woT)]):
            for r in range(HC):
                kld = self.tick("dwA")
                if self.R == "q":
                    if kld > 2:
                        self.w("vWC", kld - 2)
                    d = self.e.dma_start(out=h.wld[(kld - 1) % 2][:],
                                         in_=wsrc[128 * r:128 * (r + 1), :])
                    self.idma(d, "dwA", kld)
                kc_ = self.tick("vWC")
                if self.R == "v":
                    self.wdma("dwA", kld)
                    if kc_ > 1:
                        self.wcap(("wc", kc_ - 1))
                    i = self.e.tensor_copy(h.wcast[0][:],
                                           h.wld[(kld - 1) % 2][:])
                    self.inc(i, "vWC", 1)
                for grp in range(2):
                    pool = "PC" if (self.t["trgrp"] % 2 == 0) else "PD"
                    self.tick("trgrp")
                    kf = self.tick("f" + pool)
                    if self.R == "t":
                        self.w("vWC", kc_)
                        self.pfill(pool, kf)
                        last = None
                        for j in range(4):
                            hcj = 4 * grp + j
                            last = self.e.matmul(
                                h.psum[pool][:, 128 * j:128 * (j + 1)],
                                h.wcast[0][:, 128 * hcj:128 * (hcj + 1)],
                                h.Ib[:], start=True, stop=True)
                        self.inc(last, "f" + pool, 1)
                    ke = self.tick("e" + pool)
                    if self.R == "v":
                        self.w("f" + pool, kf)
                        dst = wdst.rearrange("p (hc o) -> p hc o", o=H)[
                            :, 4 * grp:4 * grp + 4, 128 * r:128 * (r + 1)]
                        i = self.e.tensor_copy(dst, h.psum[pool][:].rearrange(
                            "p (j o) -> p j o", o=128))
                        self.inc(i, "e" + pool, 1)
                self.cap(("wc", kc_))
        self.caps["wt_done"] = (self.t["ePC"], self.t["ePD"])
    FFN_MATS = None

    def prep_meta(self, mi):
        h = self.h
        mats = [(h.wg, h.wgTd, INNER, H), (h.wu, h.wuTd, INNER, H),
                (h.wd, h.wdTd, H, INNER)]
        return mats[mi]

    def prep_p1_tile(self, mi, ti):
        h = self.h
        wsrc, wtd, R_, C_ = self.prep_meta(mi)
        if ti == 0:
            kz = self.tick("vZabs")
            if self.R == "v":
                if mi > 0:
                    self.w("fPE", self.caps[("meanmm", mi - 1)])
                i = self.e.memset(h.sabs[:], 0.0)
                self.inc(i, "vZabs", 1)
        r, cc = ti % (R_ // 128), ti // (R_ // 128)
        src_ = wsrc[128 * r:128 * (r + 1), 1024 * cc:1024 * (cc + 1)]
        kld = self.tick("dwP")
        if self.R == "q":
            if kld > 2:
                self.w("fWP", kld - 2)
            else:
                self.w("vWC", 30 + kld)
            d = self.e.dma_start(out=h.wld[(kld - 1) % 2][:], in_=src_)
            self.idma(d, "dwP", kld)
        kv = self.tick("fWP")
        if self.R == "v":
            self.wdma("dwP", kld)
            self.e.tensor_reduce(h.babs[:], h.wld[(kld - 1) % 2][:],
                                 AX.X, ALU.add, apply_absolute_value=True)
            i = self.e.tensor_add(h.sabs[:], h.sabs[:], h.babs[:])
            self.inc(i, "fWP", 1)

    def prep_mean(self, mi):
        h = self.h
        wsrc, wtd, R_, C_ = self.prep_meta(mi)
        kf = self.tick("fPE")
        self.caps[("meanmm", mi)] = kf
        if self.R == "t":
            self.w("fWP", self.t["fWP"])
            self.pfill("PE", kf)
            mm = self.e.matmul(h.psum["PE"][0:1, 0:1], h.onesf32[:],
                               h.sabs[:], start=True, stop=True)
            self.inc(mm, "fPE", 1)
        self.tick("ePE")
        kvs = self.tick("vScale")
        if self.R == "v":
            self.w("fPE", kf)
            v = self.e
            v.tensor_scalar(h.wsc1[0:1, :], h.psum["PE"][0:1, 0:1],
                            1.0 / (R_ * C_), 1e-5, ALU.mult, ALU.max)
            i = v.reciprocal(h.wsc2[0:1, :], h.wsc1[0:1, :])
            self.inc(i, "ePE", 1)
        kf2 = self.tick("fPE")
        if self.R == "t":
            self.pfill("PE", kf2)
            self.e.matmul(h.psum["PE"][:, 0:1], h.onesf32row[:],
                          h.wsc2[0:1, :], start=True, stop=True)
            mm = self.e.matmul(h.psum["PE"][:, 1:2], h.onesf32row[:],
                               h.wsc1[0:1, :], start=True, stop=True)
            self.inc(mm, "fPE", 1)
        self.tick("ePE")
        kbc = self.tick("vBc")
        self.caps[("wsbd", mi)] = kbc
        if self.R == "v":
            self.w("fPE", kf2)
            i = self.e.tensor_copy(h.wsb[mi][:], h.psum["PE"][:, 0:1])
            self.inc(i, "vBc", 1)
            i = self.e.tensor_copy(h.dqb[mi][:], h.psum["PE"][:, 1:2])
            self.inc(i, "ePE", 1)

    def prep_p2a(self, mi, ti):
        h = self.h
        wsrc, wtd, R_, C_ = self.prep_meta(mi)
        r, cc = ti % (R_ // 128), ti // (R_ // 128)
        src_ = wsrc[128 * r:128 * (r + 1), 1024 * cc:1024 * (cc + 1)]
        kld = self.tick("dwP")
        if self.R == "q":
            if kld > 2:
                self.w("fWP", kld - 2)
            d = self.e.dma_start(out=h.wld[(kld - 1) % 2][:], in_=src_)
            self.idma(d, "dwP", kld)
        kv = self.tick("fWP")
        csl = (kv - 1) % 2
        self.caps[("p2kv", mi, ti)] = kv
        if self.R == "v":
            self.wdma("dwP", kld)
            self.wcap(("tern", kv - 2))
            self.e.tensor_scalar(h.c1s[csl][:], h.wld[(kld - 1) % 2][:],
                                 h.wsb[mi][:], 0.5, ALU.mult, ALU.is_gt)
            i = self.e.tensor_scalar(h.c2s[csl][:], h.wld[(kld - 1) % 2][:],
                                     h.wsb[mi][:], -0.5, ALU.mult, ALU.is_lt)
            self.inc(i, "fWP", 1)

    def prep_p2b(self, mi, ti):
        h = self.h
        wsrc, wtd, R_, C_ = self.prep_meta(mi)
        r, cc = ti % (R_ // 128), ti // (R_ // 128)
        kv = self.caps[("p2kv", mi, ti)]
        csl = (kv - 1) % 2
        ks_tile = self.tick("wsttile")
        for grp in range(2):
            pool = "PC" if (self.t["trgrp"] % 2 == 0) else "PD"
            self.tick("trgrp")
            kf = self.tick("f" + pool)
            if self.R == "t":
                self.w("fWP", kv)
                self.pfill(pool, kf)
                ps = h.psum[pool]
                last = None
                for j in range(4):
                    hcj = 4 * grp + j
                    c = slice(128 * hcj, 128 * (hcj + 1))
                    self.e.matmul(ps[:, 128 * j:128 * (j + 1)],
                                  h.c1s[csl][:, c], h.Ib[:],
                                  start=True, stop=False)
                    last = self.e.matmul(ps[:, 128 * j:128 * (j + 1)],
                                         h.c2s[csl][:, c], h.nIb[:],
                                         start=False, stop=True)
                self.inc(last, "f" + pool, 1)
            ke = self.tick("e" + pool)
            if self.R == "s":
                self.w("f" + pool, kf)
                if grp == 0 and ks_tile > 1:
                    self.w("dwSt", (ks_tile - 1) * 16)
                i = self.e.activation(
                    h.wtst[0][:, 512 * grp:512 * (grp + 1)],
                    h.psum[pool][:], AF.Copy)
                self.inc(i, "e" + pool, 1)
        self.cap(("tern", kv))
        ksd = self.tick("dwSt")
        if self.R == "q":
            self.w("ePC", self.t["ePC"])
            self.w("ePD", self.t["ePD"])
            dst = wtd[1024 * cc:1024 * (cc + 1),
                      128 * r:128 * (r + 1)].rearrange(
                          "(j p) i -> p j i", p=128)
            d = self.e.dma_start(
                out=dst,
                in_=h.wtst[0][:].rearrange("p (j i) -> p j i", i=128))
            self.inc(d, "dwSt", 16)

    def prep_units(self):
        units = []
        for mi in range(3):
            _, _, R_, C_ = self.prep_meta(mi)
            nt_ = (R_ // 128) * (C_ // 1024)
            units += [("p1", mi, ti) for ti in range(nt_)]
            units += [("mean", mi, 0)]
            units += [("p2a", mi, 0), ("p2a", mi, 1)]
            for ti in range(nt_ - 2):
                units += [("p2b", mi, ti), ("p2a", mi, ti + 2)]
            units += [("p2b", mi, nt_ - 2), ("p2b", mi, nt_ - 1)]
        return units

    def run_prep_units(self, n):
        while n > 0 and self._ucur < len(self._units):
            kind, mi, ti = self._units[self._ucur]
            if kind == "p1":
                self.prep_p1_tile(mi, ti)
            elif kind == "mean":
                self.prep_mean(mi)
            elif kind == "p2a":
                self.prep_p2a(mi, ti)
            else:
                self.prep_p2b(mi, ti)
            self._ucur += 1
            n -= 1

    def a1_stage0123(self, s_, g, sub=None):
        h = self.h
        if sub is None:
            sub = range(4)
        for t4 in sub:
            tg = 4 * g + t4
            i1 = self.tick("a1")
            kld = self.tick("dxa")
            if self.R == "g":
                if kld > 4:
                    self.wcap(("xtfree", kld - 4))
                d = self.e.dma_start(out=h.xt[t4][:],
                                     in_=h.x[s_, 128 * tg:128 * (tg + 1), :])
                self.idma4(d, "dxa", kld)
        for t4 in sub:
            kld = self.t["dxa"] - len(sub) + 1 + (t4 - sub[0])
            kv1 = self.tick("vSt2")
            if self.R == "v":
                self.wdma4("dxa", kld)
                if kv1 > 8:
                    self.w("sRstd", kv1 - 8)
                st = h.statsA[t4]
                v = self.e
                v.scalar_tensor_tensor(h.junkV[:, 0:1024], h.xt[t4][:], 1.0,
                                       h.xt[t4][:], ALU.mult, ALU.mult,
                                       accum_out=st[:, 1:2])
                v.reduce_sum(st[:, 0:1], h.xt[t4][:], AX.X)
                v.tensor_scalar(st[:, 2:3], st[:, 0:1], 1.0 / H, None,
                                ALU.mult)
                v.tensor_scalar(st[:, 3:4], st[:, 1:2], 1.0 / H, None,
                                ALU.mult)
                v.tensor_tensor(st[:, 4:5], st[:, 2:3], st[:, 2:3], ALU.mult)
                i = v.tensor_tensor(st[:, 4:5], st[:, 3:4], st[:, 4:5],
                                    ALU.subtract)
                self.inc(i, "vSt2", 1)
        for t4 in sub:
            kv1 = self.t["vSt2"] - len(sub) + 1 + (t4 - sub[0])
            ks2 = self.tick("sRstd")
            if self.R == "s":
                self.w("vSt2", kv1)
                st = h.statsA[t4]
                self.e.activation(st[:, 5:6], st[:, 4:5], AF.Ln,
                                  bias=h.c_eps[:])
                i = self.e.activation(st[:, 6:7], st[:, 5:6], AF.Exp,
                                      scale=-0.5)
                self.inc(i, "sRstd", 1)
        for t4 in sub:
            ks2 = self.t["sRstd"] - len(sub) + 1 + (t4 - sub[0])
            kxh = self.tick("vXH")
            if self.R == "v":
                self.w("sRstd", ks2)
                st = h.statsA[t4]
                i = self.e.tensor_scalar(h.xt[t4][:], h.xt[t4][:],
                                         st[:, 2:3], st[:, 6:7],
                                         ALU.subtract, ALU.mult)
                self.inc(i, "vXH", 1)

    def a1_stage4(self, s_, g):
        h = self.h
        tiles = [4 * g + t for t in range(4)]
        for t4, tg in enumerate(tiles):
            i1 = 16 * s_ + tg + 1
            ts4 = t4
            kxh = i1
            for grp in range(2):
                pool = "PC" if (self.t["trgrp"] % 2 == 0) else "PD"
                self.tick("trgrp")
                kf = self.tick("f" + pool)
                if self.R == "t":
                    self.w("vXH", kxh)
                    self.pfill(pool, kf)
                    last = None
                    for j in range(4):
                        hcj = 4 * grp + j
                        last = self.e.matmul(
                            h.psum[pool][:, 128 * j:128 * (j + 1)],
                            h.xt[t4][:, 128 * hcj:128 * (hcj + 1)],
                            h.Ib[:], start=True, stop=True)
                    self.inc(last, "f" + pool, 1)
                ke = self.tick("e" + pool)
                if self.R == "v":
                    self.w("f" + pool, kf)
                    dst = h.xhT4.rearrange("p (kc t c) -> p kc t c",
                                           t=4, c=128)[:, 4 * grp:4 * grp + 4,
                                                       ts4, :]
                    i = self.e.tensor_copy(dst, h.psum[pool][:].rearrange(
                        "p (j o) -> p j o", o=128))
                    self.inc(i, "e" + pool, 1)
            self.cap(("xtfree", i1))
            exht = (self.t["ePC"], self.t["ePD"])
            for og in range(2):
                pool = "PE" if (self.t["qkgrp"] % 2 == 0) else "PF"
                self.tick("qkgrp")
                kf = self.tick("f" + pool)
                if self.R == "t":
                    self.w("ePC", exht[0])
                    self.w("ePD", exht[1])
                    if i1 == 1 and og == 0:
                        self.w("ePC", self.caps["wt_done"][0])
                        self.w("ePD", self.caps["wt_done"][1])
                    self.pfill(pool, kf)
                    last = None
                    for oc4 in range(4):
                        oc = 4 * og + oc4
                        for kc in range(HC):
                            last = self.e.matmul(
                                h.psum[pool][:, 128 * oc4:128 * (oc4 + 1)],
                                h.wqT[:, H * kc + 128 * oc:
                                      H * kc + 128 * (oc + 1)],
                                h.xhT4[:, 512 * kc + 128 * ts4:
                                       512 * kc + 128 * (ts4 + 1)],
                                start=(kc == 0), stop=(kc == HC - 1))
                    self.inc(last, "f" + pool, 1)
                ke = self.tick("e" + pool)
                if self.R == "v":
                    self.w("f" + pool, kf)
                    if og == 0 and tg >= 6:
                        wp, wv = self.caps[("a2fpa", 16 * s_ + tg - 5)]
                        self.w("f" + wp, wv)
                    if og == 0 and tg == 0 and s_ > 0:
                        wp, wv = self.caps[("a2fpa", 16 * s_)]
                        self.w("f" + wp, wv)
                    dst = h.QTr.rearrange("p (hc r c) -> p hc r c",
                                          r=6, c=128)[
                        :, 4 * og:4 * og + 4, qslot(tg), :]
                    i = self.e.tensor_scalar(
                        dst,
                        h.psum[pool][:].rearrange("p (o c) -> p o c", c=128),
                        0.125, None, ALU.mult)
                    self.inc(i, "e" + pool, 1)
            kfA = self.tick("fPA")
            self.caps[("a1fpa", i1)] = kfA
            if self.R == "t":
                self.pfill("PA", kfA)
                last = None
                for half in range(2):
                    for kc in range(HC):
                        last = self.e.matmul(
                            h.psum["PA"][:, 512 * half:512 * (half + 1)],
                            h.xhT4[:, 512 * kc + 128 * ts4:
                                   512 * kc + 128 * (ts4 + 1)],
                            h.wvT[:, H * kc + 512 * half:
                                  H * kc + 512 * (half + 1)],
                            start=(kc == 0), stop=(kc == HC - 1))
                self.inc(last, "fPA", 1)
            keA = self.tick("ePA")
            self.caps[("a1epa", i1)] = keA
            if self.R == "v":
                self.w("fPA", kfA)
                if tg == 0 and s_ > 0:
                    wp, wv = self.caps[("a2fpa", 16 * s_)]
                    self.w("f" + wp, wv)
                dst = h.Vt[:, 1040 * tg:1040 * (tg + 1)].rearrange(
                    "p (hh c) -> p hh c", c=65)[:, :, 0:64]
                i = self.e.tensor_copy(dst, h.psum["PA"][:].rearrange(
                    "p (hh c) -> p hh c", c=64))
                self.inc(i, "ePA", 1)

    def k_group(self, s_, g):
        h = self.h
        for oc in range(HC):
            pool = "PE" if (self.t["qkgrp"] % 2 == 0) else "PF"
            self.tick("qkgrp")
            kf = self.tick("f" + pool)
            if self.R == "t":
                self.pfill(pool, kf)
                last = None
                for kc in range(HC):
                    last = self.e.matmul(
                        h.psum[pool][:],
                        h.wkT[:, H * kc + 128 * oc:H * kc + 128 * (oc + 1)],
                        h.xhT4[:, 512 * kc:512 * (kc + 1)],
                        start=(kc == 0), stop=(kc == HC - 1))
                self.inc(last, "f" + pool, 1)
            ke = self.tick("e" + pool)
            if self.R == "v":
                self.w("f" + pool, kf)
                if oc == 0 and g == 0 and s_ > 0:
                    wp, wv = self.caps[("a2fpa", 16 * s_)]
                    self.w("f" + wp, wv)
                dst = h.KT[:, S * oc + 512 * g:S * oc + 512 * (g + 1)]
                i = self.e.tensor_copy(dst, h.psum[pool][:])
                self.inc(i, "e" + pool, 1)
        self.caps[("kgrp", s_, g)] = (self.t["ePE"], self.t["ePF"])

    def a2_heads(self, s_, qb):
        h = self.h
        i2 = self.tick("a2")
        sl = (i2 - 1) % 2
        st = h.statsB[sl]
        kbs, masks = qb_sched(qb)
        nt = len(kbs)
        kxr = self.tick("dxr")
        if self.R == "g":
            if kxr > 2:
                self.w("vX1", kxr - 2)
            d = self.e.dma_start(out=h.xr[sl][:],
                                 in_=h.x[s_, 128 * qb:128 * (qb + 1), :])
            self.idma(d, "dxr", kxr)
        need_tile = 16 * s_ + max(max(kbs), qb) + 1
        kg = max(max(kbs), qb) // 4
        kgc = self.caps[("kgrp", s_, kg)]

        def scores_part(hh, ghead, grps):
            hc, hp = hh // 2, hh % 2
            aslots = [ghead % 4] if nt == 4 else [0, 1, 2, 3]
            for grp in grps:
                pool = "PC" if (self.t["trgrp"] % 2 == 0) else "PD"
                self.tick("trgrp")
                kf = self.tick("f" + pool)
                if self.R == "t":
                    self.w("ePA", self.caps[("a1epa", need_tile)])
                    self.w("ePE", kgc[0])
                    self.w("ePF", kgc[1])
                    self.pfill(pool, kf)
                    last = None
                    for j in range(4):
                        kb = kbs[4 * grp + j]
                        kslice = h.KT.rearrange(
                            "p (hcx t c) -> p hcx t c", t=NT, c=128)[
                            64 * hp:64 * hp + 64, hcx_ := hc, kb, :]
                        qslice = h.QTr.rearrange(
                            "p (hcx r c) -> p hcx r c", r=6, c=128)[
                            64 * hp:64 * hp + 64, hc, qslot(qb), :]
                        last = self.e.matmul(
                            h.psum[pool][:, 128 * j:128 * (j + 1)],
                            kslice, qslice, start=True, stop=True)
                    self.inc(last, "f" + pool, 1)
                ke = self.tick("e" + pool)
                asl = aslots[grp] if nt == 4 else grp
                if self.R == "s":
                    self.w("f" + pool, kf)
                    prev = self.slot_user.get(asl)
                    if prev:
                        if prev[0] == "AVa":
                            self.w("tAVa", prev[1])
                        else:
                            self.w("f" + prev[0], prev[1])
                    i = self.e.activation(h.Asb[:, 512 * asl:512 * (asl + 1)],
                                          h.psum[pool][:], AF.Exp,
                                          bias=h.cm12[:])
                    self.inc(i, "e" + pool, 1)
                kvm = self.tick("vMask")
                if self.R == "v":
                    self.w("e" + pool, ke)
                    a = h.Asb[:, 512 * asl:512 * (asl + 1)]
                    i = self.e.tensor_tensor(a, a, h.maskbuf[masks[grp]][:],
                                             ALU.mult)
                    self.inc(i, "vMask", 1)
            return aslots

        def av_part(hh, ghead, aslots):
            pool = "PE" if (self.t["avgrp"] % 2 == 0) else "PF"
            self.tick("avgrp")
            kf = self.tick("f" + pool)
            self.slot_user[aslots[0]] = (pool, kf)
            if self.R == "t":
                self.w("vMask", self.caps[("vm", ghead)])
                self.pfill(pool, kf)
                last = None
                for ki in range(nt):
                    kb = kbs[ki]
                    acol = 512 * aslots[0] + 128 * ki
                    last = self.e.matmul(
                        h.psum[pool][:, 0:65],
                        h.Asb[:, acol:acol + 128],
                        h.Vt[:, 1040 * kb + 65 * hh:1040 * kb + 65 * (hh + 1)],
                        start=(ki == 0), stop=(ki == nt - 1))
                self.inc(last, "f" + pool, 1)
            ke = self.tick("e" + pool)
            if self.R == "v":
                self.w("f" + pool, kf)
                v = self.e
                v.reciprocal(st[:, 8:9], h.psum[pool][:, 64:65])
                i = v.tensor_scalar(h.AO[:, 64 * hh:64 * (hh + 1)],
                                    h.psum[pool][:, 0:64], st[:, 8:9], None,
                                    ALU.mult)
                self.inc(i, "e" + pool, 1)

        def av16_a(hh, ghead):
            pool = "PE" if (self.t["avgrp"] % 2 == 0) else "PF"
            self.tick("avgrp")
            kf = self.tick("f" + pool)
            ka = self.tick("tAVa")
            self.slot_user[0] = ("AVa", ka)
            self.slot_user[1] = ("AVa", ka)
            if self.R == "t":
                self.w("vMask", self.caps[("vm01", ghead)])
                self.pfill(pool, kf)
                last = None
                for ki in range(8):
                    kb = kbs[ki]
                    last = self.e.matmul(
                        h.psum[pool][:, 0:65],
                        h.Asb[:, 128 * ki:128 * (ki + 1)],
                        h.Vt[:, 1040 * kb + 65 * hh:1040 * kb + 65 * (hh + 1)],
                        start=(ki == 0), stop=False)
                self.inc(last, "tAVa", 1)
            return pool, kf

        def av16_b(hh, ghead, pool, kf):
            for ii in (2, 3):
                self.slot_user[ii] = (pool, kf)
            if self.R == "t":
                self.w("vMask", self.caps[("vm23", ghead)])
                last = None
                for ki in range(8, 16):
                    kb = kbs[ki]
                    last = self.e.matmul(
                        h.psum[pool][:, 0:65],
                        h.Asb[:, 128 * ki:128 * (ki + 1)],
                        h.Vt[:, 1040 * kb + 65 * hh:1040 * kb + 65 * (hh + 1)],
                        start=False, stop=(ki == 15))
                self.inc(last, "f" + pool, 1)
            ke = self.tick("e" + pool)
            if self.R == "v":
                self.w("f" + pool, kf)
                v = self.e
                v.reciprocal(st[:, 8:9], h.psum[pool][:, 64:65])
                i = v.tensor_scalar(h.AO[:, 64 * hh:64 * (hh + 1)],
                                    h.psum[pool][:, 0:64], st[:, 8:9], None,
                                    ALU.mult)
                self.inc(i, "e" + pool, 1)

        if nt == 4:
            pend = None
            for hp in range(NH // 2):
                pair = []
                for hh in (2 * hp, 2 * hp + 1):
                    ghead = self.tick("ghead")
                    aslots = scores_part(hh, ghead, [0])
                    self.caps[("vm", ghead)] = self.t["vMask"]
                    pair.append((hh, ghead, aslots))
                if pend is not None:
                    av_part(*pend[0])
                    av_part(*pend[1])
                pend = pair
            av_part(*pend[0])
            av_part(*pend[1])
        else:
            pend = None
            for hh in range(NH):
                ghead = self.tick("ghead")
                if pend is not None:
                    ph, pg = pend
                    pp, pkf = av16_a(ph, pg)
                scores_part(hh, ghead, [0, 1])
                self.caps[("vm01", ghead)] = self.t["vMask"]
                if pend is not None:
                    av16_b(ph, pg, pp, pkf)
                scores_part(hh, ghead, [2, 3])
                self.caps[("vm23", ghead)] = self.t["vMask"]
                pend = (hh, ghead)
            ph, pg = pend
            pp, pkf = av16_a(ph, pg)
            av16_b(ph, pg, pp, pkf)
        eheads = (self.t["ePE"], self.t["ePF"])
        for grp in range(2):
            pool = "PC" if (self.t["trgrp"] % 2 == 0) else "PD"
            self.tick("trgrp")
            kf = self.tick("f" + pool)
            if self.R == "t":
                self.w("ePE", eheads[0])
                self.w("ePF", eheads[1])
                self.pfill(pool, kf)
                last = None
                for j in range(4):
                    hcj = 4 * grp + j
                    last = self.e.matmul(
                        h.psum[pool][:, 128 * j:128 * (j + 1)],
                        h.AO[:, 128 * hcj:128 * (hcj + 1)], h.Ib[:],
                        start=True, stop=True)
                self.inc(last, "f" + pool, 1)
            ke = self.tick("e" + pool)
            if self.R == "v":
                self.w("f" + pool, kf)
                if grp == 0 and i2 > 1:
                    wp, wv = self.caps[("a2wo", i2 - 1)]
                    self.w("f" + wp, wv)
                i = self.e.tensor_copy(h.AOT[:, 512 * grp:512 * (grp + 1)],
                                       h.psum[pool][:])
                self.inc(i, "e" + pool, 1)
        eaot = (self.t["ePC"], self.t["ePD"])
        wopool = "PA" if (i2 % 2 == 1) else "PBa"
        wop = h.psum["PA"] if wopool == "PA" else h.psum["PB"]
        kfA = self.tick("f" + wopool)
        self.caps[("a2fpa", i2)] = (wopool, kfA)
        self.caps[("a2wo", i2)] = (wopool, kfA)
        if self.R == "t":
            self.w("ePC", eaot[0])
            self.w("ePD", eaot[1])
            self.pfill(wopool, kfA)
            last = None
            for half in range(2):
                for hcx in range(HC):
                    last = self.e.matmul(
                        wop[:, 512 * half:512 * (half + 1)],
                        h.AOT[:, 128 * hcx:128 * (hcx + 1)],
                        h.woT[:, H * hcx + 512 * half:H * hcx + 512 * (half + 1)],
                        start=(hcx == 0), stop=(hcx == HC - 1))
            self.inc(last, "f" + wopool, 1)
        return (i2, sl, kxr, wopool, kfA)

    def a2_epi(self, ctx):
        h = self.h
        i2, sl, kxr, wopool, kfA = ctx
        st = h.statsB[sl]
        wop = h.psum["PA"] if wopool == "PA" else h.psum["PB"]
        self.tick("e" + wopool)
        ksp = self.tick("sProj")
        ks2 = self.tick("sRstd2")
        if self.R == "s":
            self.w("f" + wopool, kfA)
            if ksp > 1:
                self.w("vX1", ksp - 1)
            s = self.e
            i = s.activation(h.proj[0][:], wop[:], AF.Identity,
                             bias=h.czero[:], accum_out=st[:, 0:1])
            self.inc(i, "e" + wopool, 1)
            s.activation(h.ysb[0][:], h.proj[0][:], AF.Square,
                         accum_out=st[:, 1:2])
            s.activation(st[:, 2:3], st[:, 0:1], AF.Identity,
                         scale=1.0 / H, bias=h.czero[:])
            s.activation(st[:, 3:4], st[:, 1:2], AF.Identity,
                         scale=1.0 / H, bias=h.czero[:])
            s.activation(st[:, 4:5], st[:, 2:3], AF.Square, bias=h.czero[:])
            s.activation(st[:, 5:6], st[:, 4:5], AF.Identity, scale=-1.0,
                         bias=st[:, 3:4])
            s.activation(st[:, 6:7], st[:, 5:6], AF.Ln, bias=h.c_eps[:])
            i = s.activation(st[:, 7:8], st[:, 6:7], AF.Exp, scale=-0.5)
            self.inc(i, "sRstd2", 1)
        kx1 = self.tick("vX1")
        if self.R == "v":
            self.w("sRstd2", ks2)
            self.wdma("dxr", kxr)
            if kx1 > 2:
                self.wdma("dx1w", kx1 - 2)
            self.e.tensor_scalar(h.ysb[0][:], h.proj[0][:], st[:, 2:3],
                                 st[:, 7:8], ALU.subtract, ALU.mult)
            i = self.e.tensor_tensor(h.x1sl[sl][:], h.xr[sl][:], h.ysb[0][:],
                                     ALU.add)
            self.inc(i, "vX1", 1)
        kw = self.tick("dx1w")
        if self.R == "s":
            self.w("vX1", kx1)
            s_, qb = self.caps[("rowof", i2)]
            row = 2048 * s_ + 128 * qb
            d = self.e.dma_start(out=h.x1d[row:row + 128, :],
                                 in_=h.x1sl[sl][:])
            self.idma(d, "dx1w", kw)

    def phase_A(self):
        self.slot_user = {}
        self._units = self.prep_units()
        self._ucur = 0
        groups = [(s_, g) for s_ in range(NSEQ) for g in range(4)]
        self.a1_stage0123(*groups[0])
        pend = None
        for idx, (s_, g) in enumerate(groups):
            self.a1_stage4(s_, g)
            self.k_group(s_, g)
            self.run_prep_units(2)
            nqb = len(A2_BATCH[g])
            subs = {2: [[0, 1], [2, 3]],
                    4: [[0], [1], [2], [3]],
                    6: [[0], [1], [2], [3], [], []]}[nqb]
            for qbi, qb in enumerate(A2_BATCH[g]):
                i2next = self.t["a2"] + 1
                self.caps[("rowof", i2next)] = (s_, qb)
                ctx = self.a2_heads(s_, qb)
                if pend is not None:
                    self.a2_epi(pend)
                pend = ctx
                if idx + 1 < len(groups) and subs[qbi]:
                    self.a1_stage0123(*groups[idx + 1], sub=subs[qbi])
                self.run_prep_units(3)
            if g == 3:
                self.a2_epi(pend)
                pend = None
        self.run_prep_units(10 ** 6)

    def barrier_AB(self):
        caps = [("fPA", self.t["fPA"]), ("ePA", self.t["ePA"]),
                ("fPC", self.t["fPC"]), ("fPD", self.t["fPD"]),
                ("ePC", self.t["ePC"]), ("ePD", self.t["ePD"]),
                ("fPE", self.t["fPE"]), ("fPF", self.t["fPF"]),
                ("ePE", self.t["ePE"]), ("ePF", self.t["ePF"]),
                ("fPBa", self.t["fPBa"]), ("ePBa", self.t["ePBa"]),
                ("fPBb", self.t["fPBb"]), ("ePBb", self.t["ePBb"]),
                ("vX1", self.t["vX1"]), ("vXH", self.t["vXH"]),
                ("sRstd2", self.t["sRstd2"])]
        nw = self.t["dx1w"]
        caps += [("dx1w0", ((nw + 1) // 2) * 16), ("dx1w1", (nw // 2) * 16)]
        nxa = self.t["dxa"]
        for sli in range(4):
            caps += [("dxa%d" % sli, ((nxa - sli + 3) // 4) * 16)]
        nxr = self.t["dxr"]
        caps += [("dxr0", ((nxr + 1) // 2) * 16), ("dxr1", (nxr // 2) * 16)]
        for sem, val in caps:
            self.w(sem, val)

    def phase_B(self):
        h = self.h
        self.barrier_AB()
        self.tick("dwB")
        self.tick("dwB")
        if self.R == "s":
            self.w("dwSt", self.t["dwSt"] * 16)
            d1 = self.e.dma_start(
                out=h.wu_sb[:].rearrange("p (kc i) -> p kc i", i=INNER),
                in_=h.wuTd.rearrange("(kc p) i -> p kc i", p=128))
            self.inc(d1, "dwB", 16)
            d2 = self.e.dma_start(
                out=h.wd_sb[:].rearrange("p (ic o) -> p ic o", o=H),
                in_=h.wdTd.rearrange("(ic p) o -> p ic o", p=128))
            self.inc(d2, "dwB", 16)
        self.b_quant_chain(0)
        self.b_quant_chain_b(0)
        self.b_quant_transpose(0)
        self._tail = None
        for sg in range(NSG):
            hold = {}

            def _head0():
                kvh = self.bdown_head_v(sg, 0)
                hold["k0"] = self.bdown_head_s(sg, 0, kvh)
                self.down_tail()

            for icb in range(ICB - 1):
                self.b_gate_up(sg, icb)
            self.b_gate_up(sg, ICB - 1, headfn=_head0)
            self.caps[("gu_done", sg)] = (self.t["fPF"], self.t["fPBb"])
            kvh1 = self.bdown_head_v(sg, 1)
            if sg + 1 < NSG:
                self.b_quant_chain(sg + 1)

            def _head1():
                hold["k1"] = self.bdown_head_s(sg, 1, kvh1)

            self.bdown_body(sg, 0, SGT * sg, hold["k0"], headfn=_head1)
            if sg + 1 < NSG:
                self.b_quant_chain_b(sg + 1)
            self.bdown_body(sg, 1, SGT * sg + 1, hold["k1"],
                            defer=(sg + 1 < NSG))
            if sg + 1 < NSG:
                self.b_quant_transpose(sg + 1)
        self.tick("fin")
        if self.R == "q":
            n = self.t["dout"]
            self.w("dout0", ((n + 1) // 2) * 16)
            self.w("dout1", (n // 2) * 16)

    def b_quant_chain(self, sg):
        h = self.h
        klds = []
        for i in range(SGT):
            t_ = SGT * sg + i
            self.tick("bq")
            kld = self.tick("dx1r")
            klds.append(kld)
            if self.R == "q":
                nsp = 8 * (t_ // 16 + 1)
                self.w("dx1w0", nsp * 16)
                self.w("dx1w1", nsp * 16)
                if kld > 4:
                    self.w("vOut", kld - 4)
                d = self.e.dma_start(out=h.x1t[(sg % 2) * 2 + i][:],
                                     in_=h.x1d[128 * t_:128 * (t_ + 1), :])
                self.idma(d, "dx1r", kld)
        for i in range(SGT):
            st = h.statsC[i]
            kv2 = self.tick("vBSt2")
            if self.R == "v":
                self.wdma("dx1r", klds[i])
                if kv2 > SGT:
                    self.w("sBRstd", kv2 - SGT)
                v = self.e
                xs = h.x1t[(sg % 2) * 2 + i]
                v.scalar_tensor_tensor(h.junkV[:, 0:1024], xs[:], 1.0,
                                       xs[:], ALU.mult, ALU.mult,
                                       accum_out=st[:, 1:2])
                v.reduce_sum(st[:, 0:1], xs[:], AX.X)
                v.reduce_max(st[:, 2:3], xs[:], AX.X)
                v.tensor_reduce(st[:, 3:4], xs[:], AX.X, ALU.min)
                v.tensor_scalar(st[:, 4:5], st[:, 0:1], 1.0 / H, None,
                                ALU.mult)
                v.tensor_scalar(st[:, 5:6], st[:, 1:2], 1.0 / H, None,
                                ALU.mult)
                v.tensor_tensor(st[:, 6:7], st[:, 4:5], st[:, 4:5], ALU.mult)
                v.tensor_tensor(st[:, 6:7], st[:, 5:6], st[:, 6:7],
                                ALU.subtract)
                v.tensor_tensor(st[:, 7:8], st[:, 2:3], st[:, 4:5],
                                ALU.subtract)
                v.tensor_tensor(st[:, 8:9], st[:, 4:5], st[:, 3:4],
                                ALU.subtract)
                i_ = v.tensor_tensor(st[:, 7:8], st[:, 7:8], st[:, 8:9],
                                     ALU.max)
                self.inc(i_, "vBSt2", 1)
        return klds

    def b_quant_chain_b(self, sg):
        h = self.h
        for i in range(SGT):
            st = h.statsC[i]
            kv2 = self.t["vBSt2"] - SGT + 1 + i
            ks2 = self.tick("sBRstd")
            if self.R == "s":
                self.w("vBSt2", kv2)
                self.e.activation(st[:, 9:10], st[:, 6:7], AF.Ln,
                                  bias=h.c_eps[:])
                i_ = self.e.activation(st[:, 10:11], st[:, 9:10], AF.Exp,
                                       scale=-0.5)
                self.inc(i_, "sBRstd", 1)
        for i in range(SGT):
            st = h.statsC[i]
            ks2 = self.t["sBRstd"] - SGT + 1 + i
            ksu = self.tick("sU")
            self.tick("vBQs")
            if self.R == "v":
                self.w("sBRstd", ks2)
                v = self.e
                v.tensor_tensor(st[:, 11:12], st[:, 7:8], st[:, 10:11],
                                ALU.mult)
                v.tensor_scalar(st[:, 11:12], st[:, 11:12], 1e-5, None,
                                ALU.max)
                v.reciprocal(st[:, 12:13], st[:, 11:12])
                v.tensor_scalar(st[:, 12:13], st[:, 12:13], 127.0, None,
                                ALU.mult)
                v.tensor_tensor(st[:, 13:14], st[:, 10:11], st[:, 12:13],
                                ALU.mult)
                v.scalar_tensor_tensor(st[:, 14:15], st[:, 4:5], -1.0,
                                       st[:, 13:14], ALU.mult, ALU.mult)
                v.tensor_scalar(st[:, 14:15], st[:, 14:15], MAGIC, None,
                                ALU.add)
                v.tensor_scalar(st[:, 15:16], st[:, 11:12], 1.0 / 127.0,
                                None, ALU.mult)
                v.tensor_tensor(st[:, 16:17], st[:, 15:16], h.dqb[0][:],
                                ALU.mult)
                i_ = v.tensor_tensor(st[:, 17:18], st[:, 15:16],
                                     h.dqb[1][:], ALU.mult)
                self.inc(i_, "vBQs", 1)
            til = (ksu + 1) // 2
            if self.R == "v":
                if til > 2:
                    self.wcap(("uq", til - 2))
                usl = (til - 1) % 2
                c = slice(0, 512)
                i_ = self.e.tensor_scalar(h.uact[usl][:, c],
                                          h.x1t[(sg % 2) * 2 + i][:, c],
                                          st[:, 13:14], st[:, 14:15],
                                          ALU.mult, ALU.add)
                self.inc(i_, "sU", 1)
            ksu2 = self.tick("sU")
            if self.R == "v":
                usl = (til - 1) % 2
                c = slice(512, 1024)
                i_ = self.e.tensor_scalar(h.uact[usl][:, c],
                                          h.x1t[(sg % 2) * 2 + i][:, c],
                                          st[:, 13:14], st[:, 14:15],
                                          ALU.mult, ALU.add)
                self.inc(i_, "sU", 1)

    def b_quant_transpose(self, sg):
        h = self.h
        for i in range(SGT):
            ksu_a = self.t["sU"] - 2 * SGT + 1 + 2 * i
            til = (ksu_a + 1) // 2
            for grp in range(2):
                pool = "PC" if (self.t["trgrp"] % 2 == 0) else "PD"
                self.tick("trgrp")
                kf = self.tick("f" + pool)
                if self.R == "t":
                    self.w("sU", ksu_a + grp)
                    self.pfill(pool, kf)
                    ps = h.psum[pool]
                    last = None
                    for j in range(4):
                        hcj = 4 * grp + j
                        last = self.e.matmul(
                            ps[:, 128 * j:128 * (j + 1)],
                            h.uact[(til - 1) % 2][:, 128 * hcj:128 * (hcj + 1)],
                            h.I16[:], start=True, stop=True)
                    self.inc(last, "f" + pool, 1)
                ke = self.tick("e" + pool)
                if self.R == "v":
                    self.w("f" + pool, kf)
                    if grp == 0 and sg >= 1 and i == 0:
                        gdc = self.caps[("gu_done", sg - 1)]
                        self.w("fPF", gdc[0])
                        self.w("fPBb", gdc[1])
                    i_ = self.e.tensor_scalar(
                        h.xqT[i][:, 512 * grp:512 * (grp + 1)],
                        h.psum[pool][:], -MAGIC, None, ALU.add)
                    self.inc(i_, "e" + pool, 1)
            self.cap(("uq", til))
            self.caps[("xq", SGT * sg + i + 1)] = (self.t["ePC"],
                                                   self.t["ePD"])

    def b_gate_up(self, sg, icb, headfn=None):
        h = self.h
        kwg = self.tick("dwg")
        if self.R == "q":
            self.w("dwSt", 32 * 16)
            if kwg > 2:
                gpp, gpv = self.caps[("gchain", kwg - 2)]
                self.w("f" + gpp, gpv)
            src = h.wgTd[:, 512 * icb:512 * (icb + 1)].rearrange(
                "(kc p) n -> p kc n", p=128)
            d = self.e.dma_start(
                out=h.wgsl[(kwg - 1) % 2][:].rearrange("p (kc n) -> p kc n",
                                                       n=512),
                in_=src)
            self.idma(d, "dwg", kwg)
        for i in range(SGT):
            t_ = SGT * sg + i
            par = self.t["gup"] % 2
            self.tick("gup")
            gp = ("PE", "PBa")[par]
            up = ("PF", "PBb")[par]
            kfg = self.tick("f" + gp)
            kfu = self.tick("f" + up)
            if self.R == "t":
                self.wdma("dwg", kwg)
                self.w("dwB", 32)
                cxq = self.caps[("xq", t_ + 1)]
                self.w("ePC", cxq[0])
                self.w("ePD", cxq[1])
                self.pfill(gp, kfg)
                last = None
                for kc in range(HC):
                    last = self.e.matmul(
                        self.ppsum(gp),
                        h.xqT[i][:, 128 * kc:128 * (kc + 1)],
                        h.wgsl[(kwg - 1) % 2][:, 512 * kc:512 * (kc + 1)],
                        start=(kc == 0), stop=(kc == HC - 1))
                self.inc(last, "f" + gp, 1)
                self.pfill(up, kfu)
                for kc in range(HC):
                    last = self.e.matmul(
                        self.ppsum(up),
                        h.xqT[i][:, 128 * kc:128 * (kc + 1)],
                        h.wu_sb[:, INNER * kc + 512 * icb:
                                INNER * kc + 512 * (icb + 1)],
                        start=(kc == 0), stop=(kc == HC - 1))
                self.inc(last, "f" + up, 1)
            if i == SGT - 1:
                self.caps[("gchain", kwg)] = (gp, kfg)
            keg = self.tick("e" + gp)
            ksg = self.tick("sGs")
            if self.R == "s":
                self.w("f" + gp, kfg)
                prevh = self.caps.get(("hprev", ksg - 1))
                if prevh:
                    self.w("e" + prevh[0], prevh[1])
                st = h.statsC[i]
                i_ = self.e.activation(h.gs_sb[:], self.ppsum(gp), AF.Silu,
                                       scale=st[:, 16:17], bias=h.czero[:])
                self.inc(i_, "e" + gp, 1)
            keu = self.tick("e" + up)
            self.caps[("hprev", ksg)] = (up, keu)
            if self.R == "v":
                self.w("e" + gp, keg)
                self.w("f" + up, kfu)
                if sg > 0 and icb == 0:
                    self.w("sUh", SGT * (sg - 1) + i + 1)
                st = h.statsC[i]
                self.e.scalar_tensor_tensor(
                    h.Hb[i][:, 512 * icb:512 * (icb + 1)], h.gs_sb[:],
                    st[:, 17:18], self.ppsum(up), ALU.mult, ALU.mult)
                i_ = self.e.tensor_reduce(
                    h.hpart[i][:, icb:icb + 1],
                    h.Hb[i][:, 512 * icb:512 * (icb + 1)], AX.X, ALU.max,
                    apply_absolute_value=True)
                self.inc(i_, "e" + up, 1)
            if i == 0 and headfn is not None:
                headfn()

    def emit_down(self, rec, icg, kfD, pooldown):
        h = self.h
        pool, ke, kvq = rec
        if self.R == "t":
            self.w("e" + pool, ke)
            if icg == 0:
                self.pfill(pooldown, kfD)
            sl2 = (kvq - 1) % 4
            last = None
            for j in range(4):
                icj = 4 * icg + j
                for half in range(2):
                    last = self.e.matmul(
                        h.psum[pooldown][:, 512 * half:512 * (half + 1)],
                        h.hqT[sl2][:, 128 * j:128 * (j + 1)],
                        h.wd_sb[:, 1024 * icj + 512 * half:
                                1024 * icj + 512 * (half + 1)],
                        start=(icj == 0), stop=(icj == IC - 1))
            if icg == 7:
                self.inc(last, "f" + pooldown, 1)

    def bdown_head_v(self, sg, i):
        h = self.h
        st = h.statsC[i]
        o = 0
        kvh = self.tick("vHmax")
        if self.R == "v":
            v = self.e
            v.tensor_reduce(st[:, 18+o:19+o], h.hpart[i][:], AX.X, ALU.max)
            v.tensor_scalar(st[:, 18+o:19+o], st[:, 18+o:19+o], 1e-5, None,
                            ALU.max)
            v.reciprocal(st[:, 19+o:20+o], st[:, 18+o:19+o])
            v.tensor_scalar(st[:, 19+o:20+o], st[:, 19+o:20+o], 127.0, None,
                            ALU.mult)
            v.tensor_scalar(st[:, 20+o:21+o], st[:, 18+o:19+o], 1.0 / 127.0,
                            None, ALU.mult)
            i_ = v.tensor_tensor(st[:, 21+o:22+o], st[:, 20+o:21+o],
                                 h.dqb[2][:], ALU.mult)
            self.inc(i_, "vHmax", 1)
        return kvh

    def bdown_head_s(self, sg, i, kvh):
        h = self.h
        st = h.statsC[i]
        last = None
        for q in range(2):
            ksq = self.tick("sUh")
            if self.R == "s":
                if q == 0:
                    self.w("vHmax", kvh)
                    if ksq > 2:
                        self.wcap(("uhq", (ksq - 1) // 2))
                c = slice(2048 * q, 2048 * (q + 1))
                i_ = self.e.activation(h.uh[:, c], h.Hb[i][:, c],
                                       AF.Identity, scale=st[:, 19:20],
                                       bias=h.b1536[:])
                self.inc(i_, "sUh", 1)
            last = ksq
        return last

    def bdown_body(self, sg, i, t_, ksu, headfn=None, defer=False):
        h = self.h
        st = h.statsC[i]
        x1slot = (sg % 2) * 2 + i
        pooldown = "PA"
        kfD = self.tick("f" + pooldown)
        trrec = []
        for icg in range(8):
            pool = ("PC", "PD", "PE", "PF")[self.tick("btr") % 4]
            self.tick("trgrp")
            kf = self.tick("f" + pool)
            if self.R == "t":
                self.w("sUh", ksu - 1 if icg < 4 else ksu)
                self.pfill(pool, kf)
                ps = h.psum[pool]
                last = None
                for j in range(4):
                    icj = 4 * icg + j
                    last = self.e.matmul(ps[:, 128 * j:128 * (j + 1)],
                                         h.uh[:, 128 * icj:128 * (icj + 1)],
                                         h.I16[:], start=True, stop=True)
                self.inc(last, "f" + pool, 1)
            ke = self.tick("e" + pool)
            kvq = self.tick("vHq")
            if self.R == "s":
                self.w("f" + pool, kf)
                i_ = self.e.activation(h.hqT[(kvq - 1) % 4][:],
                                       h.psum[pool][:], AF.Identity,
                                       bias=h.cm1536[:])
                self.inc(i_, "e" + pool, 1)
            trrec.append((pool, ke, kvq))
            if icg >= 3:
                self.emit_down(trrec[icg - 3], icg - 3, kfD, pooldown)
        self.cap(("uhq", ksu // 2))
        if headfn is not None:
            headfn()
        if defer:
            self._tail = (trrec, kfD, sg, i, t_)
            return
        self.down_tail_emit(trrec, kfD, sg, i, t_)

    TAIL0 = 5

    def down_tail(self):
        if self._tail is None:
            return
        trrec, kfD, sg, i, t_ = self._tail
        self._tail = None
        self.down_tail_emit(trrec, kfD, sg, i, t_)

    def down_tail_emit(self, trrec, kfD, sg, i, t_):
        h = self.h
        st = h.statsC[i]
        x1slot = (sg % 2) * 2 + i
        pooldown = "PA"
        for icg in range(self.TAIL0, 8):
            self.emit_down(trrec[icg], icg, kfD, pooldown)
        keD = self.tick("e" + pooldown)
        ksd = self.tick("sD")
        dsl = (ksd - 1) % 2
        if self.R == "s":
            self.w("f" + pooldown, kfD)
            if ksd > 2:
                self.w("vOut", ksd - 2)
            i_ = self.e.activation(h.d_sb[dsl][:], h.psum[pooldown][:],
                                   AF.Identity, scale=st[:, 21:22],
                                   bias=h.czero[:], accum_out=st[:, 22:23])
            self.inc(i_, "e" + pooldown, 1)
        kvs = self.tick("vDSq")
        if self.R == "v":
            self.w("ePA", keD)
            v = self.e
            v.scalar_tensor_tensor(h.junkV[:], h.d_sb[dsl][:], 1.0,
                                   h.d_sb[dsl][:], ALU.mult, ALU.mult,
                                   accum_out=st[:, 23:24])
            v.tensor_scalar(st[:, 24:25], st[:, 22:23], 1.0 / H, None,
                            ALU.mult)
            v.tensor_scalar(st[:, 25:26], st[:, 23:24], 1.0 / H, None,
                            ALU.mult)
            v.tensor_tensor(st[:, 26:27], st[:, 24:25], st[:, 24:25],
                            ALU.mult)
            i_ = v.tensor_tensor(st[:, 26:27], st[:, 25:26], st[:, 26:27],
                                 ALU.subtract)
            self.inc(i_, "vDSq", 1)
        ks3 = self.tick("sDR")
        if self.R == "s":
            self.w("vDSq", kvs)
            s = self.e
            s.activation(st[:, 27:28], st[:, 26:27], AF.Ln, bias=h.c_eps[:])
            i_ = s.activation(st[:, 28:29], st[:, 27:28], AF.Exp, scale=-0.5)
            self.inc(i_, "sDR", 1)
        kout = self.tick("vOut")
        if self.R == "v":
            self.w("sDR", ks3)
            if kout > 2:
                self.wdma("dout", kout - 2)
            sl2 = (kout - 1) % 2
            self.e.tensor_scalar(h.d_sb[dsl][:], h.d_sb[dsl][:],
                                 st[:, 24:25], st[:, 28:29], ALU.subtract,
                                 ALU.mult)
            i_ = self.e.tensor_tensor(h.out_sb[sl2][:], h.x1t[x1slot][:],
                                      h.d_sb[dsl][:], ALU.add)
            self.inc(i_, "vOut", 1)
        kst = self.tick("dout")
        if self.R == "s":
            self.w("vOut", kout)
            sl2 = (kout - 1) % 2
            d = self.e.dma_start(out=h.yflat[128 * t_:128 * (t_ + 1), :],
                                 in_=h.out_sb[sl2][:])
            self.idma(d, "dout", kst)

    def emit_down(self, rec, icg, kfD, pooldown):
        h = self.h
        pool, ke, kvq = rec
        if self.R == "t":
            self.w("e" + pool, ke)
            if icg == 0:
                self.pfill(pooldown, kfD)
            sl2 = (kvq - 1) % 4
            last = None
            for j in range(4):
                icj = 4 * icg + j
                for half in range(2):
                    last = self.e.matmul(
                        h.psum[pooldown][:, 512 * half:512 * (half + 1)],
                        h.hqT[sl2][:, 128 * j:128 * (j + 1)],
                        h.wd_sb[:, 1024 * icj + 512 * half:
                                1024 * icj + 512 * (half + 1)],
                        start=(icj == 0), stop=(icj == IC - 1))
            if icg == 7:
                self.inc(last, "f" + pooldown, 1)

class Handles:
    pass


def build_and_emit():
    nc = bass.Bass("TRN2", target_bir_lowering=False)
    h = Handles()
    h.x = nc.dram_tensor("x", [NSEQ, S, H], F32, kind="ExternalInput").ap()
    h.wq = nc.dram_tensor("wq", [H, H], F32, kind="ExternalInput").ap()
    h.wk = nc.dram_tensor("wk", [H, H], F32, kind="ExternalInput").ap()
    h.wv = nc.dram_tensor("wv", [H, H], F32, kind="ExternalInput").ap()
    h.wo = nc.dram_tensor("wo", [H, H], F32, kind="ExternalInput").ap()
    h.wg = nc.dram_tensor("wg", [INNER, H], F32, kind="ExternalInput").ap()
    h.wu = nc.dram_tensor("wu", [INNER, H], F32, kind="ExternalInput").ap()
    h.wd = nc.dram_tensor("wd", [H, INNER], F32, kind="ExternalInput").ap()
    h.y = nc.dram_tensor("y", [NSEQ, S, H], F32, kind="ExternalOutput").ap()
    h.yflat = h.y.rearrange("s t h -> (s t) h")
    h.x1d = nc.dram_tensor("x1d", [NSEQ * S, H], BF16, kind="Internal").ap()
    h.wgTd = nc.dram_tensor("wgTd", [H, INNER], BF16, kind="Internal").ap()
    h.wuTd = nc.dram_tensor("wuTd", [H, INNER], BF16, kind="Internal").ap()
    h.wdTd = nc.dram_tensor("wdTd", [INNER, H], BF16, kind="Internal").ap()

    es0 = ExitStack()
    sb = lambda name, shape, dt: es0.enter_context(
        nc.sbuf_tensor(name, shape, dt))
    h.Ib = sb("Ib", [128, 128], BF16)
    h.nIb = sb("nIb", [128, 128], BF16)
    h.I16 = sb("I16", [128, 128], FP16)
    h.m1536row = sb("m1536row", [1, 128], FP16)
    h.ones16row = sb("ones16row", [1, 128], FP16)
    h.onesf32 = sb("onesf32", [128, 1], F32)
    h.onesf32row = sb("onesf32row", [1, 128], F32)
    h.b1536 = sb("b1536", [128, 1], F32)
    h.cm12 = sb("cm12", [128, 1], F32)
    h.cm1536 = sb("cm1536", [128, 1], F32)
    h.c_eps = sb("c_eps", [128, 1], F32)
    h.czero = sb("czero", [128, 1], F32)
    h.junkV = sb("junkV", [128, 1024], BF16)
    h.statsA = [sb(f"statsA{i}", [128, 8], F32) for i in range(4)]
    h.statsB = [sb(f"statsB{i}", [128, 12], F32) for i in range(2)]
    h.statsC = [sb(f"statsC{i}", [128, 32], F32) for i in range(SGT)]
    h.sabs = sb("sabs", [128, 1], F32)
    h.babs = sb("babs", [128, 1], F32)
    h.wsc1 = sb("wsc1", [1, 1], F32)
    h.wsc2 = sb("wsc2", [1, 1], F32)
    h.wsb = [sb(f"wsb{i}", [128, 1], F32) for i in range(3)]
    h.dqb = [sb(f"dqb{i}", [128, 1], F32) for i in range(3)]

    esp = ExitStack()
    h.psum = {}
    h.psum["PA"] = esp.enter_context(nc.psum_tensor("PA", [128, 1024], F32))
    h.psum["PB"] = esp.enter_context(nc.psum_tensor("PB", [128, 1024], F32))
    for p in ("PC", "PD", "PE", "PF"):
        h.psum[p] = esp.enter_context(nc.psum_tensor(p, [128, 512], F32))

    ess = ExitStack()
    h.sems = {}
    semnames = ["gc", "gser",
                "dwA0", "dwA1", "dwP0", "dwP1", "dxa0", "dxa1", "dxa2", "dxa3", "vXH",
                "dxr0", "dxr1", "dx1w0", "dx1w1", "dx1r0", "dx1r1",
                "dwg0", "dwg1", "dout0", "dout1", "dwSt", "dwB",
                "vWC", "fWP", "vZabs", "vScale", "vBc",
                "vSt1", "sSq", "vSt2", "sRstd", "vBias", "sXH",
                "vMask", "sProj", "vPSq", "sRstd2", "vBias2", "sY", "vX1",
                "vBSt1", "sBSq", "vBSt2", "sBRstd", "vBQs", "sU", "vXq",
                "sGs", "vHmax", "sUh", "sD", "vDSq", "sDR", "vDB", "sYd",
                "vOut", "vser", "sser", "tAVa", "gp1",
                "fPA", "ePA", "fPBa", "ePBa", "fPBb", "ePBb", "fPC", "ePC", "fPD", "ePD",
                "fPE", "ePE", "fPF", "ePF"]
    for sname in semnames:
        h.sems[sname] = ess.enter_context(nc.semaphore(sname))

    esA = ExitStack()
    sbA = lambda name, shape, dt: esA.enter_context(
        nc.sbuf_tensor(name, shape, dt))
    h.maskbuf = {}
    for mname in ("QB0A", "C0x4", "QB1", "MID", "QB15"):
        h.maskbuf[mname] = sbA("M_" + mname, [128, 512], BF16)
    h.M_QB0A = h.maskbuf["QB0A"]; h.M_C0x4 = h.maskbuf["C0x4"]
    h.M_QB1 = h.maskbuf["QB1"]; h.M_MID = h.maskbuf["MID"]
    h.M_QB15 = h.maskbuf["QB15"]
    h.wqT = sbA("wqT", [128, HC * H], BF16)
    h.wkT = sbA("wkT", [128, HC * H], BF16)
    h.wvT = sbA("wvT", [128, HC * H], BF16)
    h.woT = sbA("woT", [128, HC * H], BF16)
    h.wld = [sbA(f"wld{i}", [128, 1024], F32) for i in range(2)]
    h.wcast = [sbA("wcast0", [128, 1024], BF16)]
    h.c1s = [sbA(f"c1s{i}", [128, 1024], BF16) for i in range(2)]
    h.c2s = [sbA(f"c2s{i}", [128, 1024], BF16) for i in range(2)]
    h.wtst = [sbA("wtst0", [128, 1024], BF16)]
    h.QTr = sbA("QTr", [128, HC * 6 * 128], BF16)
    h.KT = sbA("KT", [128, HC * S], BF16)
    h.Vt = sbA("Vt", [128, NT * (NH * 65)], BF16)
    h.xt = [sbA(f"xt{i}", [128, 1024], BF16) for i in range(4)]
    h.xhT4 = sbA("xhT4", [128, HC * 512], BF16)
    h.xr = [sbA(f"xr{i}", [128, 1024], BF16) for i in range(2)]
    h.x1sl = [sbA(f"x1sl{i}", [128, 1024], BF16) for i in range(2)]
    h.Asb = sbA("Asb", [128, 2048], BF16)
    h.AO = sbA("AO", [128, 1024], BF16)
    h.AOT = sbA("AOT", [128, 1024], BF16)
    h.proj = [sbA("proj0", [128, 1024], BF16)]
    h.ysb = [sbA("ysb0", [128, 1024], BF16)]

    counts = {}
    capsd = {}
    sercnts = {}
    with nc.Block() as block:
        def emit_role(role, eng):
            wlk = Walk(h, role, eng)
            wlk.phase_consts()
            wlk.phase_prep()
            wlk.phase_A()
            capsd[role] = wlk.caps
            if isinstance(wlk.e, ChainEng):
                sercnts[role] = wlk.e.cnt
            return dict(wlk.t)

        @block.tensor
        def _(t):
            counts["t"] = emit_role("t", t)

        @block.vector
        def _(v):
            counts["v"] = emit_role("v", v)

        @block.scalar
        def _(s):
            counts["s"] = emit_role("s", s)

        @block.sync
        def _(q):
            counts["q"] = emit_role("q", q)

        @block.gpsimd
        def _(g):
            counts["g"] = emit_role("g", g)
    base = counts["t"]
    for r in "vsqg":
        assert counts[r] == base, f"tick divergence (A) role {r}"
        assert capsd[r] == capsd["t"], f"caps divergence (A) role {r}"

    esA.close()
    esB = ExitStack()
    sbB = lambda name, shape, dt: esB.enter_context(
        nc.sbuf_tensor(name, shape, dt))
    h.wu_sb = sbB("wu_sb", [128, HC * INNER], BF16)
    h.wd_sb = sbB("wd_sb", [128, IC * H], BF16)
    h.wgsl = [sbB(f"wgsl{i}", [128, HC * 512], BF16) for i in range(2)]
    h.Hb = [sbB(f"Hb{i}", [128, INNER], BF16) for i in range(SGT)]
    h.xqT = [sbB(f"xqT{i}", [128, 1024], BF16) for i in range(SGT)]
    h.x1t = [sbB(f"x1t{i}", [128, 1024], BF16) for i in range(4)]
    h.uact = [sbB(f"uact{i}", [128, 1024], FP16) for i in range(2)]
    h.uh = sbB("uh", [128, INNER], FP16)
    h.hqT = [sbB(f"hqT{i}", [128, 512], BF16) for i in range(4)]
    h.gs_sb = sbB("gs_sb", [128, 512], BF16)
    h.d_sb = [sbB(f"d_sb{i}", [128, 1024], BF16) for i in range(2)]
    
    h.out_sb = [sbB(f"out_sb{i}", [128, 1024], F32) for i in range(2)]
    h.hpart = [sbB(f"hpart{i}", [128, 8], F32) for i in range(SGT)]

    counts2 = {}
    caps2 = {}
    with nc.Block() as block:
        def emit_roleB(role, eng):
            wlk = Walk(h, role, eng, counts[role],
                       sercnt=sercnts.get(role, 0))
            wlk.caps = dict(capsd[role])
            wlk.phase_B()
            caps2[role] = wlk.caps
            return dict(wlk.t)

        @block.tensor
        def _(t):
            counts2["t"] = emit_roleB("t", t)

        @block.vector
        def _(v):
            counts2["v"] = emit_roleB("v", v)

        @block.scalar
        def _(s):
            counts2["s"] = emit_roleB("s", s)

        @block.sync
        def _(q):
            counts2["q"] = emit_roleB("q", q)

        @block.gpsimd
        def _(g):
            counts2["g"] = emit_roleB("g", g)
    base = counts2["t"]
    for r in "vsqg":
        assert counts2[r] == base, f"tick divergence (B) role {r}"
        assert caps2[r] == caps2["t"], f"caps divergence (B) role {r}"
    esB.close()
    ess.close()
    esp.close()
    es0.close()
    return nc


_PROGRAM_CACHE = {}


def _get_program():
    if "nc" not in _PROGRAM_CACHE:
        _PROGRAM_CACHE["nc"] = build_and_emit()
    return _PROGRAM_CACHE["nc"]


def kernel(L0_attn_x, L0_attn_Wq, L0_attn_Wk, L0_attn_Wv, L0_attn_Wo,
           L0_ffn_w_gate, L0_ffn_w_up, L0_ffn_w_down):
    nc = _get_program()
    x = np.ascontiguousarray(np.asarray(L0_attn_x, dtype=np.float32))
    args = {
        "wq": np.ascontiguousarray(np.asarray(L0_attn_Wq, dtype=np.float32)),
        "wk": np.ascontiguousarray(np.asarray(L0_attn_Wk, dtype=np.float32)),
        "wv": np.ascontiguousarray(np.asarray(L0_attn_Wv, dtype=np.float32)),
        "wo": np.ascontiguousarray(np.asarray(L0_attn_Wo, dtype=np.float32)),
        "wg": np.ascontiguousarray(np.asarray(L0_ffn_w_gate, dtype=np.float32)),
        "wu": np.ascontiguousarray(np.asarray(L0_ffn_w_up, dtype=np.float32)),
        "wd": np.ascontiguousarray(np.asarray(L0_ffn_w_down, dtype=np.float32)),
    }
    in_maps = []
    for c in range(NCORES):
        m = {"x": np.ascontiguousarray(x[NSEQ * c:NSEQ * (c + 1)])}
        m.update(args)
        in_maps.append(m)
    res = run_bass_kernel_spmd(nc, in_maps, core_ids=list(range(NCORES)))
    return np.concatenate([r["y"] for r in res.results], axis=0)


# version 38 (final)
# speedup vs baseline: 1.0123x; 1.0123x over previous
import numpy as np
from collections import defaultdict
from contextlib import ExitStack

import concourse.bass as bass
import concourse.mybir as mybir
from concourse.bass_utils import run_bass_kernel_spmd

F32 = mybir.dt.float32
BF16 = mybir.dt.bfloat16
FP16 = mybir.dt.float16
AF = mybir.ActivationFunctionType
ALU = mybir.AluOpType
AX = mybir.AxisListType

NCORES = 8
NSEQ = 4
S = 2048
NT = 16
H = 1024
HC = 8
NH = 16
HD = 64
INNER = 4096
IC = 32
ICB = 8
SGT = 2
NSG = (NSEQ * NT) // SGT
NTILE = NSEQ * NT
EPS = 1e-5
EXPOFF = -12.0
MAGIC = 1536.0


def qb_sched(qb):
    if qb == 0:
        return list(range(16)), ["QB0A", "C0x4", "C0x4", "C0x4"]
    if qb == 1:
        return [0, 1, 2, 0], ["QB1"]
    if qb <= 14:
        return [qb - 1, qb, qb + 1, 0], ["MID"]
    return [14, 15, 0, 13], ["QB15"]


def qslot(tg):
    return 5 if tg == 0 else tg % 5


A2_BATCH = {0: [1, 2], 1: [3, 4, 5, 6], 2: [7, 8, 9, 10],
            3: [11, 12, 13, 14, 15, 0]}


class ChainEng:

    PASS = ("wait_ge", "dma_start", "nop", "engine_nop")

    def __init__(self, wk, eng, ser, cnt0=0):
        self.wk = wk
        self.raw = eng
        self.ser = ser
        self.pend = None
        self.cnt = cnt0

    def __getattr__(self, name):
        real = getattr(self.raw, name)
        if name in ChainEng.PASS or not callable(real):
            return real

        def wrapped(*a, **k):
            self._settle_and_wait()
            instr = real(*a, **k)
            self.pend = instr
            return instr

        return wrapped

    def _settle_and_wait(self):
        wk, p = self.wk, self.pend
        if p is None:
            return
        rec = wk._edge_of.get(id(p))
        if rec is not None and rec[0] is p:
            edge = rec[1:]
        else:
            self.cnt += 1
            p.then_inc(wk.h.sems[self.ser], 1)
            edge = (self.ser, self.cnt)
            wk._edge_of[id(p)] = (p, self.ser, self.cnt)
        self.raw.wait_ge(wk.h.sems[edge[0]], edge[1])


class Walk:

    def __init__(self, hs, role, eng, counts=None, sercnt=0):
        self.h = hs
        self.R = role
        self._edge_of = {}
        if role == "v":
            eng = ChainEng(self, eng, "vser", sercnt)
        elif role == "s":
            eng = ChainEng(self, eng, "sser", sercnt)
        self.e = eng
        self.t = defaultdict(int, counts or {})
        self.caps = {}

    def tick(self, name):
        self.t[name] += 1
        return self.t[name]

    def inc(self, instr, sem, n=1):
        assert instr is not getattr(self, "_last", None), f"double inc {sem}"
        instr.then_inc(self.h.sems[sem], n)
        self._last = instr
        if n == 1:
            self._edge_of[id(instr)] = (instr, sem, self.t[sem])
        else:
            self._edge_of[id(instr)] = (instr, sem, -1)
        return instr

    def w(self, sem, val):
        if val > 0:
            self.e.wait_ge(self.h.sems[sem], val)

    def wdma(self, base, k):
        self.w(base + str((k - 1) % 2), ((k - 1) // 2 + 1) * 16)

    def idma(self, d, base, k):
        self.inc(d, base + str((k - 1) % 2), 16)

    def wdma4(self, base, k):
        self.w(base + str((k - 1) % 4), ((k - 1) // 4 + 1) * 16)

    def idma4(self, d, base, k):
        self.inc(d, base + str((k - 1) % 4), 16)

    def pfill(self, pool, kf):
        if kf > 1:
            self.w("e" + pool, kf - 1)

    def cap(self, key):
        self.caps[key] = (self.t["fPC"], self.t["fPD"])

    def ppsum(self, name):
        h = self.h
        if name == "PBa":
            return h.psum["PB"][:, 0:512]
        if name == "PBb":
            return h.psum["PB"][:, 512:1024]
        return h.psum[name][:]

    def wcap(self, key):
        c = self.caps.get(key)
        if c:
            self.w("fPC", c[0])
            self.w("fPD", c[1])

    def phase_consts(self):
        h = self.h
        if self.R == "g":
            g = self.e
            _k = [0]

            def gq(instr):
                _k[0] += 1
                instr.then_inc(h.sems["gser"], 1)
                g.wait_ge(h.sems["gser"], _k[0])

            for buf, neg in ((h.Ib, False), (h.nIb, True), (h.I16, False)):
                gq(g.memset(buf[:], -1.0 if neg else 1.0))
                gq(g.affine_select(buf[:], buf[:], [[-1, 128]], ALU.is_equal,
                                   0.0, channel_multiplier=1))
            gq(g.memset(h.m1536row[:], -MAGIC))
            gq(g.memset(h.ones16row[:], 1.0))
            gq(g.memset(h.onesf32[:], 1.0))
            gq(g.memset(h.onesf32row[:], 1.0))
            gq(g.memset(h.b1536[:], MAGIC))
            gq(g.memset(h.cm12[:], EXPOFF))
            gq(g.memset(h.cm1536[:], -MAGIC))
            gq(g.memset(h.c_eps[:], EPS))
            gq(g.memset(h.czero[:], 0.0))

            def sec(buf, j):
                return buf[:, 128 * j:128 * (j + 1)]

            def lt(bs):
                gq(g.memset(bs, 1.0))
                gq(g.affine_select(bs, bs, [[-1, 128]], ALU.is_ge, 0.0,
                                   channel_multiplier=1))

            def ut(bs):
                gq(g.memset(bs, 1.0))
                gq(g.affine_select(bs, bs, [[1, 128]], ALU.is_ge, 0.0,
                                   channel_multiplier=-1))

            gq(g.memset(sec(h.M_QB0A, 0), 1.0))
            ut(sec(h.M_QB0A, 1))
            gq(g.memset(sec(h.M_QB0A, 1)[:, 0:1], 1.0))
            for j in (2, 3):
                gq(g.memset(sec(h.M_QB0A, j), 0.0))
                gq(g.memset(sec(h.M_QB0A, j)[:, 0:1], 1.0))
            for j in range(4):
                gq(g.memset(sec(h.M_C0x4, j), 0.0))
                gq(g.memset(sec(h.M_C0x4, j)[:, 0:1], 1.0))
            lt(sec(h.M_QB1, 0))
            gq(g.memset(sec(h.M_QB1, 0)[0:1, :], 1.0))
            gq(g.memset(sec(h.M_QB1, 1), 1.0))
            ut(sec(h.M_QB1, 2))
            gq(g.memset(sec(h.M_QB1, 3), 0.0))
            lt(sec(h.M_MID, 0))
            gq(g.memset(sec(h.M_MID, 1), 1.0))
            ut(sec(h.M_MID, 2))
            gq(g.memset(sec(h.M_MID, 3), 0.0))
            gq(g.memset(sec(h.M_MID, 3)[0:1, :], 1.0))
            lt(sec(h.M_QB15, 0))
            gq(g.memset(sec(h.M_QB15, 1), 1.0))
            gq(g.memset(sec(h.M_QB15, 2), 0.0))
            gq(g.memset(sec(h.M_QB15, 2)[0:1, :], 1.0))
            gq(g.memset(sec(h.M_QB15, 3), 0.0))
            i = g.memset(h.Vt[:], 1.0)
            self.inc(i, "gc", 1)
        self.tick("gc")
        self.w("gc", 1)

    def phase_prep(self):
        h = self.h
        for mi, (wsrc, wdst) in enumerate(
                [(h.wq, h.wqT), (h.wk, h.wkT), (h.wv, h.wvT), (h.wo, h.woT)]):
            for r in range(HC):
                kld = self.tick("dwA")
                if self.R == "q":
                    if kld > 2:
                        self.w("vWC", kld - 2)
                    d = self.e.dma_start(out=h.wld[(kld - 1) % 2][:],
                                         in_=wsrc[128 * r:128 * (r + 1), :])
                    self.idma(d, "dwA", kld)
                kc_ = self.tick("vWC")
                if self.R == "v":
                    self.wdma("dwA", kld)
                    if kc_ > 1:
                        self.wcap(("wc", kc_ - 1))
                    i = self.e.tensor_copy(h.wcast[0][:],
                                           h.wld[(kld - 1) % 2][:])
                    self.inc(i, "vWC", 1)
                for grp in range(2):
                    pool = "PC" if (self.t["trgrp"] % 2 == 0) else "PD"
                    self.tick("trgrp")
                    kf = self.tick("f" + pool)
                    if self.R == "t":
                        self.w("vWC", kc_)
                        self.pfill(pool, kf)
                        last = None
                        for j in range(4):
                            hcj = 4 * grp + j
                            last = self.e.matmul(
                                h.psum[pool][:, 128 * j:128 * (j + 1)],
                                h.wcast[0][:, 128 * hcj:128 * (hcj + 1)],
                                h.Ib[:], start=True, stop=True)
                        self.inc(last, "f" + pool, 1)
                    ke = self.tick("e" + pool)
                    if self.R == "v":
                        self.w("f" + pool, kf)
                        dst = wdst.rearrange("p (hc o) -> p hc o", o=H)[
                            :, 4 * grp:4 * grp + 4, 128 * r:128 * (r + 1)]
                        i = self.e.tensor_copy(dst, h.psum[pool][:].rearrange(
                            "p (j o) -> p j o", o=128))
                        self.inc(i, "e" + pool, 1)
                self.cap(("wc", kc_))
        self.caps["wt_done"] = (self.t["ePC"], self.t["ePD"])
    FFN_MATS = None

    def prep_meta(self, mi):
        h = self.h
        mats = [(h.wg, h.wgTd, INNER, H), (h.wu, h.wuTd, INNER, H),
                (h.wd, h.wdTd, H, INNER)]
        return mats[mi]

    def prep_p1_tile(self, mi, ti):
        h = self.h
        wsrc, wtd, R_, C_ = self.prep_meta(mi)
        if ti == 0:
            kz = self.tick("vZabs")
            if self.R == "v":
                if mi > 0:
                    self.w("fPE", self.caps[("meanmm", mi - 1)])
                i = self.e.memset(h.sabs[:], 0.0)
                self.inc(i, "vZabs", 1)
        r, cc = ti % (R_ // 128), ti // (R_ // 128)
        src_ = wsrc[128 * r:128 * (r + 1), 1024 * cc:1024 * (cc + 1)]
        kld = self.tick("dwP")
        if self.R == "q":
            if kld > 2:
                self.w("fWP", kld - 2)
            else:
                self.w("vWC", 30 + kld)
            d = self.e.dma_start(out=h.wld[(kld - 1) % 2][:], in_=src_)
            self.idma(d, "dwP", kld)
        kv = self.tick("fWP")
        if self.R == "v":
            self.wdma("dwP", kld)
            self.e.tensor_reduce(h.babs[:], h.wld[(kld - 1) % 2][:],
                                 AX.X, ALU.add, apply_absolute_value=True)
            i = self.e.tensor_add(h.sabs[:], h.sabs[:], h.babs[:])
            self.inc(i, "fWP", 1)

    def prep_mean(self, mi):
        h = self.h
        wsrc, wtd, R_, C_ = self.prep_meta(mi)
        kf = self.tick("fPE")
        self.caps[("meanmm", mi)] = kf
        if self.R == "t":
            self.w("fWP", self.t["fWP"])
            self.pfill("PE", kf)
            mm = self.e.matmul(h.psum["PE"][0:1, 0:1], h.onesf32[:],
                               h.sabs[:], start=True, stop=True)
            self.inc(mm, "fPE", 1)
        self.tick("ePE")
        kvs = self.tick("vScale")
        if self.R == "v":
            self.w("fPE", kf)
            v = self.e
            v.tensor_scalar(h.wsc1[0:1, :], h.psum["PE"][0:1, 0:1],
                            1.0 / (R_ * C_), 1e-5, ALU.mult, ALU.max)
            i = v.reciprocal(h.wsc2[0:1, :], h.wsc1[0:1, :])
            self.inc(i, "ePE", 1)
        kf2 = self.tick("fPE")
        if self.R == "t":
            self.pfill("PE", kf2)
            self.e.matmul(h.psum["PE"][:, 0:1], h.onesf32row[:],
                          h.wsc2[0:1, :], start=True, stop=True)
            mm = self.e.matmul(h.psum["PE"][:, 1:2], h.onesf32row[:],
                               h.wsc1[0:1, :], start=True, stop=True)
            self.inc(mm, "fPE", 1)
        self.tick("ePE")
        kbc = self.tick("vBc")
        self.caps[("wsbd", mi)] = kbc
        if self.R == "v":
            self.w("fPE", kf2)
            i = self.e.tensor_copy(h.wsb[mi][:], h.psum["PE"][:, 0:1])
            self.inc(i, "vBc", 1)
            i = self.e.tensor_copy(h.dqb[mi][:], h.psum["PE"][:, 1:2])
            self.inc(i, "ePE", 1)

    def prep_p2a(self, mi, ti):
        h = self.h
        wsrc, wtd, R_, C_ = self.prep_meta(mi)
        r, cc = ti % (R_ // 128), ti // (R_ // 128)
        src_ = wsrc[128 * r:128 * (r + 1), 1024 * cc:1024 * (cc + 1)]
        kld = self.tick("dwP")
        if self.R == "q":
            if kld > 2:
                self.w("fWP", kld - 2)
            d = self.e.dma_start(out=h.wld[(kld - 1) % 2][:], in_=src_)
            self.idma(d, "dwP", kld)
        kv = self.tick("fWP")
        csl = (kv - 1) % 2
        self.caps[("p2kv", mi, ti)] = kv
        if self.R == "v":
            self.wdma("dwP", kld)
            self.wcap(("tern", kv - 2))
            self.e.tensor_scalar(h.c1s[csl][:], h.wld[(kld - 1) % 2][:],
                                 h.wsb[mi][:], 0.5, ALU.mult, ALU.is_gt)
            i = self.e.tensor_scalar(h.c2s[csl][:], h.wld[(kld - 1) % 2][:],
                                     h.wsb[mi][:], -0.5, ALU.mult, ALU.is_lt)
            self.inc(i, "fWP", 1)

    def prep_p2b(self, mi, ti):
        h = self.h
        wsrc, wtd, R_, C_ = self.prep_meta(mi)
        r, cc = ti % (R_ // 128), ti // (R_ // 128)
        kv = self.caps[("p2kv", mi, ti)]
        csl = (kv - 1) % 2
        ks_tile = self.tick("wsttile")
        for grp in range(2):
            pool = "PC" if (self.t["trgrp"] % 2 == 0) else "PD"
            self.tick("trgrp")
            kf = self.tick("f" + pool)
            if self.R == "t":
                self.w("fWP", kv)
                self.pfill(pool, kf)
                ps = h.psum[pool]
                last = None
                for j in range(4):
                    hcj = 4 * grp + j
                    c = slice(128 * hcj, 128 * (hcj + 1))
                    self.e.matmul(ps[:, 128 * j:128 * (j + 1)],
                                  h.c1s[csl][:, c], h.Ib[:],
                                  start=True, stop=False)
                    last = self.e.matmul(ps[:, 128 * j:128 * (j + 1)],
                                         h.c2s[csl][:, c], h.nIb[:],
                                         start=False, stop=True)
                self.inc(last, "f" + pool, 1)
            ke = self.tick("e" + pool)
            if self.R == "s":
                self.w("f" + pool, kf)
                if grp == 0 and ks_tile > 1:
                    self.w("dwSt", (ks_tile - 1) * 16)
                i = self.e.activation(
                    h.wtst[0][:, 512 * grp:512 * (grp + 1)],
                    h.psum[pool][:], AF.Copy)
                self.inc(i, "e" + pool, 1)
        self.cap(("tern", kv))
        ksd = self.tick("dwSt")
        if self.R == "q":
            self.w("ePC", self.t["ePC"])
            self.w("ePD", self.t["ePD"])
            dst = wtd[1024 * cc:1024 * (cc + 1),
                      128 * r:128 * (r + 1)].rearrange(
                          "(j p) i -> p j i", p=128)
            d = self.e.dma_start(
                out=dst,
                in_=h.wtst[0][:].rearrange("p (j i) -> p j i", i=128))
            self.inc(d, "dwSt", 16)

    def prep_units(self):
        units = []
        for mi in range(3):
            _, _, R_, C_ = self.prep_meta(mi)
            nt_ = (R_ // 128) * (C_ // 1024)
            units += [("p1", mi, ti) for ti in range(nt_)]
            units += [("mean", mi, 0)]
            units += [("p2a", mi, 0), ("p2a", mi, 1)]
            for ti in range(nt_ - 2):
                units += [("p2b", mi, ti), ("p2a", mi, ti + 2)]
            units += [("p2b", mi, nt_ - 2), ("p2b", mi, nt_ - 1)]
        return units

    def run_prep_units(self, n):
        while n > 0 and self._ucur < len(self._units):
            kind, mi, ti = self._units[self._ucur]
            if kind == "p1":
                self.prep_p1_tile(mi, ti)
            elif kind == "mean":
                self.prep_mean(mi)
            elif kind == "p2a":
                self.prep_p2a(mi, ti)
            else:
                self.prep_p2b(mi, ti)
            self._ucur += 1
            n -= 1

    def a1_stage0123(self, s_, g, sub=None):
        h = self.h
        if sub is None:
            sub = range(4)
        for t4 in sub:
            tg = 4 * g + t4
            i1 = self.tick("a1")
            kld = self.tick("dxa")
            if self.R == "g":
                if kld > 4:
                    self.wcap(("xtfree", kld - 4))
                d = self.e.dma_start(out=h.xt[t4][:],
                                     in_=h.x[s_, 128 * tg:128 * (tg + 1), :])
                self.idma4(d, "dxa", kld)
        for t4 in sub:
            kld = self.t["dxa"] - len(sub) + 1 + (t4 - sub[0])
            kv1 = self.tick("vSt2")
            if self.R == "v":
                self.wdma4("dxa", kld)
                if kv1 > 8:
                    self.w("sRstd", kv1 - 8)
                st = h.statsA[t4]
                v = self.e
                v.scalar_tensor_tensor(h.junkV[:, 0:1024], h.xt[t4][:], 1.0,
                                       h.xt[t4][:], ALU.mult, ALU.mult,
                                       accum_out=st[:, 1:2])
                v.reduce_sum(st[:, 0:1], h.xt[t4][:], AX.X)
                v.tensor_scalar(st[:, 2:3], st[:, 0:1], 1.0 / H, None,
                                ALU.mult)
                v.tensor_scalar(st[:, 3:4], st[:, 1:2], 1.0 / H, None,
                                ALU.mult)
                v.tensor_tensor(st[:, 4:5], st[:, 2:3], st[:, 2:3], ALU.mult)
                i = v.tensor_tensor(st[:, 4:5], st[:, 3:4], st[:, 4:5],
                                    ALU.subtract)
                self.inc(i, "vSt2", 1)
        for t4 in sub:
            kv1 = self.t["vSt2"] - len(sub) + 1 + (t4 - sub[0])
            ks2 = self.tick("sRstd")
            if self.R == "s":
                self.w("vSt2", kv1)
                st = h.statsA[t4]
                self.e.activation(st[:, 5:6], st[:, 4:5], AF.Ln,
                                  bias=h.c_eps[:])
                i = self.e.activation(st[:, 6:7], st[:, 5:6], AF.Exp,
                                      scale=-0.5)
                self.inc(i, "sRstd", 1)
        for t4 in sub:
            ks2 = self.t["sRstd"] - len(sub) + 1 + (t4 - sub[0])
            kxh = self.tick("vXH")
            if self.R == "v":
                self.w("sRstd", ks2)
                st = h.statsA[t4]
                i = self.e.tensor_scalar(h.xt[t4][:], h.xt[t4][:],
                                         st[:, 2:3], st[:, 6:7],
                                         ALU.subtract, ALU.mult)
                self.inc(i, "vXH", 1)

    def a1_stage4(self, s_, g):
        h = self.h
        tiles = [4 * g + t for t in range(4)]
        for t4, tg in enumerate(tiles):
            i1 = 16 * s_ + tg + 1
            ts4 = t4
            kxh = i1
            for grp in range(2):
                pool = "PC" if (self.t["trgrp"] % 2 == 0) else "PD"
                self.tick("trgrp")
                kf = self.tick("f" + pool)
                if self.R == "t":
                    self.w("vXH", kxh)
                    self.pfill(pool, kf)
                    last = None
                    for j in range(4):
                        hcj = 4 * grp + j
                        last = self.e.matmul(
                            h.psum[pool][:, 128 * j:128 * (j + 1)],
                            h.xt[t4][:, 128 * hcj:128 * (hcj + 1)],
                            h.Ib[:], start=True, stop=True)
                    self.inc(last, "f" + pool, 1)
                ke = self.tick("e" + pool)
                if self.R == "v":
                    self.w("f" + pool, kf)
                    dst = h.xhT4.rearrange("p (kc t c) -> p kc t c",
                                           t=4, c=128)[:, 4 * grp:4 * grp + 4,
                                                       ts4, :]
                    i = self.e.tensor_copy(dst, h.psum[pool][:].rearrange(
                        "p (j o) -> p j o", o=128))
                    self.inc(i, "e" + pool, 1)
            self.cap(("xtfree", i1))
            exht = (self.t["ePC"], self.t["ePD"])
            for og in range(2):
                pool = "PE" if (self.t["qkgrp"] % 2 == 0) else "PF"
                self.tick("qkgrp")
                kf = self.tick("f" + pool)
                if self.R == "t":
                    self.w("ePC", exht[0])
                    self.w("ePD", exht[1])
                    if i1 == 1 and og == 0:
                        self.w("ePC", self.caps["wt_done"][0])
                        self.w("ePD", self.caps["wt_done"][1])
                    self.pfill(pool, kf)
                    last = None
                    for oc4 in range(4):
                        oc = 4 * og + oc4
                        for kc in range(HC):
                            last = self.e.matmul(
                                h.psum[pool][:, 128 * oc4:128 * (oc4 + 1)],
                                h.wqT[:, H * kc + 128 * oc:
                                      H * kc + 128 * (oc + 1)],
                                h.xhT4[:, 512 * kc + 128 * ts4:
                                       512 * kc + 128 * (ts4 + 1)],
                                start=(kc == 0), stop=(kc == HC - 1))
                    self.inc(last, "f" + pool, 1)
                ke = self.tick("e" + pool)
                if self.R == "v":
                    self.w("f" + pool, kf)
                    if og == 0 and tg >= 6:
                        wp, wv = self.caps[("a2fpa", 16 * s_ + tg - 5)]
                        self.w("f" + wp, wv)
                    if og == 0 and tg == 0 and s_ > 0:
                        wp, wv = self.caps[("a2fpa", 16 * s_)]
                        self.w("f" + wp, wv)
                    dst = h.QTr.rearrange("p (hc r c) -> p hc r c",
                                          r=6, c=128)[
                        :, 4 * og:4 * og + 4, qslot(tg), :]
                    i = self.e.tensor_scalar(
                        dst,
                        h.psum[pool][:].rearrange("p (o c) -> p o c", c=128),
                        0.125, None, ALU.mult)
                    self.inc(i, "e" + pool, 1)
            kfA = self.tick("fPA")
            self.caps[("a1fpa", i1)] = kfA
            if self.R == "t":
                self.pfill("PA", kfA)
                last = None
                for half in range(2):
                    for kc in range(HC):
                        last = self.e.matmul(
                            h.psum["PA"][:, 512 * half:512 * (half + 1)],
                            h.xhT4[:, 512 * kc + 128 * ts4:
                                   512 * kc + 128 * (ts4 + 1)],
                            h.wvT[:, H * kc + 512 * half:
                                  H * kc + 512 * (half + 1)],
                            start=(kc == 0), stop=(kc == HC - 1))
                self.inc(last, "fPA", 1)
            keA = self.tick("ePA")
            self.caps[("a1epa", i1)] = keA
            if self.R == "v":
                self.w("fPA", kfA)
                if tg == 0 and s_ > 0:
                    wp, wv = self.caps[("a2fpa", 16 * s_)]
                    self.w("f" + wp, wv)
                dst = h.Vt[:, 1040 * tg:1040 * (tg + 1)].rearrange(
                    "p (hh c) -> p hh c", c=65)[:, :, 0:64]
                i = self.e.tensor_copy(dst, h.psum["PA"][:].rearrange(
                    "p (hh c) -> p hh c", c=64))
                self.inc(i, "ePA", 1)

    def k_group(self, s_, g):
        h = self.h
        for oc in range(HC):
            pool = "PE" if (self.t["qkgrp"] % 2 == 0) else "PF"
            self.tick("qkgrp")
            kf = self.tick("f" + pool)
            if self.R == "t":
                self.pfill(pool, kf)
                last = None
                for kc in range(HC):
                    last = self.e.matmul(
                        h.psum[pool][:],
                        h.wkT[:, H * kc + 128 * oc:H * kc + 128 * (oc + 1)],
                        h.xhT4[:, 512 * kc:512 * (kc + 1)],
                        start=(kc == 0), stop=(kc == HC - 1))
                self.inc(last, "f" + pool, 1)
            ke = self.tick("e" + pool)
            if self.R == "v":
                self.w("f" + pool, kf)
                if oc == 0 and g == 0 and s_ > 0:
                    wp, wv = self.caps[("a2fpa", 16 * s_)]
                    self.w("f" + wp, wv)
                dst = h.KT[:, S * oc + 512 * g:S * oc + 512 * (g + 1)]
                i = self.e.tensor_copy(dst, h.psum[pool][:])
                self.inc(i, "e" + pool, 1)
        self.caps[("kgrp", s_, g)] = (self.t["ePE"], self.t["ePF"])

    def a2_heads(self, s_, qb):
        h = self.h
        i2 = self.tick("a2")
        sl = (i2 - 1) % 2
        st = h.statsB[sl]
        kbs, masks = qb_sched(qb)
        nt = len(kbs)
        kxr = self.tick("dxr")
        if self.R == "g":
            if kxr > 2:
                self.w("vX1", kxr - 2)
            d = self.e.dma_start(out=h.xr[sl][:],
                                 in_=h.x[s_, 128 * qb:128 * (qb + 1), :])
            self.idma(d, "dxr", kxr)
        need_tile = 16 * s_ + max(max(kbs), qb) + 1
        kg = max(max(kbs), qb) // 4
        kgc = self.caps[("kgrp", s_, kg)]

        def scores_part(hh, ghead, grps):
            hc, hp = hh // 2, hh % 2
            aslots = [ghead % 4] if nt == 4 else [0, 1, 2, 3]
            for grp in grps:
                pool = "PC" if (self.t["trgrp"] % 2 == 0) else "PD"
                self.tick("trgrp")
                kf = self.tick("f" + pool)
                if self.R == "t":
                    self.w("ePA", self.caps[("a1epa", need_tile)])
                    self.w("ePE", kgc[0])
                    self.w("ePF", kgc[1])
                    self.pfill(pool, kf)
                    last = None
                    for j in range(4):
                        kb = kbs[4 * grp + j]
                        kslice = h.KT.rearrange(
                            "p (hcx t c) -> p hcx t c", t=NT, c=128)[
                            64 * hp:64 * hp + 64, hcx_ := hc, kb, :]
                        qslice = h.QTr.rearrange(
                            "p (hcx r c) -> p hcx r c", r=6, c=128)[
                            64 * hp:64 * hp + 64, hc, qslot(qb), :]
                        last = self.e.matmul(
                            h.psum[pool][:, 128 * j:128 * (j + 1)],
                            kslice, qslice, start=True, stop=True)
                    self.inc(last, "f" + pool, 1)
                ke = self.tick("e" + pool)
                asl = aslots[grp] if nt == 4 else grp
                if self.R == "s":
                    self.w("f" + pool, kf)
                    prev = self.slot_user.get(asl)
                    if prev:
                        if prev[0] == "AVa":
                            self.w("tAVa", prev[1])
                        else:
                            self.w("f" + prev[0], prev[1])
                    i = self.e.activation(h.Asb[:, 512 * asl:512 * (asl + 1)],
                                          h.psum[pool][:], AF.Exp,
                                          bias=h.cm12[:])
                    self.inc(i, "e" + pool, 1)
                kvm = self.tick("vMask")
                if self.R == "v":
                    self.w("e" + pool, ke)
                    a = h.Asb[:, 512 * asl:512 * (asl + 1)]
                    i = self.e.tensor_tensor(a, a, h.maskbuf[masks[grp]][:],
                                             ALU.mult)
                    self.inc(i, "vMask", 1)
            return aslots

        def av_part(hh, ghead, aslots):
            pool = "PE" if (self.t["avgrp"] % 2 == 0) else "PF"
            self.tick("avgrp")
            kf = self.tick("f" + pool)
            self.slot_user[aslots[0]] = (pool, kf)
            if self.R == "t":
                self.w("vMask", self.caps[("vm", ghead)])
                self.pfill(pool, kf)
                last = None
                for ki in range(nt):
                    kb = kbs[ki]
                    acol = 512 * aslots[0] + 128 * ki
                    last = self.e.matmul(
                        h.psum[pool][:, 0:65],
                        h.Asb[:, acol:acol + 128],
                        h.Vt[:, 1040 * kb + 65 * hh:1040 * kb + 65 * (hh + 1)],
                        start=(ki == 0), stop=(ki == nt - 1))
                self.inc(last, "f" + pool, 1)
            ke = self.tick("e" + pool)
            if self.R == "v":
                self.w("f" + pool, kf)
                v = self.e
                v.reciprocal(st[:, 8:9], h.psum[pool][:, 64:65])
                i = v.tensor_scalar(h.AO[:, 64 * hh:64 * (hh + 1)],
                                    h.psum[pool][:, 0:64], st[:, 8:9], None,
                                    ALU.mult)
                self.inc(i, "e" + pool, 1)

        def av16_a(hh, ghead):
            pool = "PE" if (self.t["avgrp"] % 2 == 0) else "PF"
            self.tick("avgrp")
            kf = self.tick("f" + pool)
            ka = self.tick("tAVa")
            self.slot_user[0] = ("AVa", ka)
            self.slot_user[1] = ("AVa", ka)
            if self.R == "t":
                self.w("vMask", self.caps[("vm01", ghead)])
                self.pfill(pool, kf)
                last = None
                for ki in range(8):
                    kb = kbs[ki]
                    last = self.e.matmul(
                        h.psum[pool][:, 0:65],
                        h.Asb[:, 128 * ki:128 * (ki + 1)],
                        h.Vt[:, 1040 * kb + 65 * hh:1040 * kb + 65 * (hh + 1)],
                        start=(ki == 0), stop=False)
                self.inc(last, "tAVa", 1)
            return pool, kf

        def av16_b(hh, ghead, pool, kf):
            for ii in (2, 3):
                self.slot_user[ii] = (pool, kf)
            if self.R == "t":
                self.w("vMask", self.caps[("vm23", ghead)])
                last = None
                for ki in range(8, 16):
                    kb = kbs[ki]
                    last = self.e.matmul(
                        h.psum[pool][:, 0:65],
                        h.Asb[:, 128 * ki:128 * (ki + 1)],
                        h.Vt[:, 1040 * kb + 65 * hh:1040 * kb + 65 * (hh + 1)],
                        start=False, stop=(ki == 15))
                self.inc(last, "f" + pool, 1)
            ke = self.tick("e" + pool)
            if self.R == "v":
                self.w("f" + pool, kf)
                v = self.e
                v.reciprocal(st[:, 8:9], h.psum[pool][:, 64:65])
                i = v.tensor_scalar(h.AO[:, 64 * hh:64 * (hh + 1)],
                                    h.psum[pool][:, 0:64], st[:, 8:9], None,
                                    ALU.mult)
                self.inc(i, "e" + pool, 1)

        if nt == 4:
            pend = None
            for hp in range(NH // 2):
                pair = []
                for hh in (2 * hp, 2 * hp + 1):
                    ghead = self.tick("ghead")
                    aslots = scores_part(hh, ghead, [0])
                    self.caps[("vm", ghead)] = self.t["vMask"]
                    pair.append((hh, ghead, aslots))
                if pend is not None:
                    av_part(*pend[0])
                    av_part(*pend[1])
                pend = pair
            av_part(*pend[0])
            av_part(*pend[1])
        else:
            pend = None
            for hh in range(NH):
                ghead = self.tick("ghead")
                if pend is not None:
                    ph, pg = pend
                    pp, pkf = av16_a(ph, pg)
                scores_part(hh, ghead, [0, 1])
                self.caps[("vm01", ghead)] = self.t["vMask"]
                if pend is not None:
                    av16_b(ph, pg, pp, pkf)
                scores_part(hh, ghead, [2, 3])
                self.caps[("vm23", ghead)] = self.t["vMask"]
                pend = (hh, ghead)
            ph, pg = pend
            pp, pkf = av16_a(ph, pg)
            av16_b(ph, pg, pp, pkf)
        eheads = (self.t["ePE"], self.t["ePF"])
        for grp in range(2):
            pool = "PC" if (self.t["trgrp"] % 2 == 0) else "PD"
            self.tick("trgrp")
            kf = self.tick("f" + pool)
            if self.R == "t":
                self.w("ePE", eheads[0])
                self.w("ePF", eheads[1])
                self.pfill(pool, kf)
                last = None
                for j in range(4):
                    hcj = 4 * grp + j
                    last = self.e.matmul(
                        h.psum[pool][:, 128 * j:128 * (j + 1)],
                        h.AO[:, 128 * hcj:128 * (hcj + 1)], h.Ib[:],
                        start=True, stop=True)
                self.inc(last, "f" + pool, 1)
            ke = self.tick("e" + pool)
            if self.R == "v":
                self.w("f" + pool, kf)
                if grp == 0 and i2 > 1:
                    wp, wv = self.caps[("a2wo", i2 - 1)]
                    self.w("f" + wp, wv)
                i = self.e.tensor_copy(h.AOT[:, 512 * grp:512 * (grp + 1)],
                                       h.psum[pool][:])
                self.inc(i, "e" + pool, 1)
        eaot = (self.t["ePC"], self.t["ePD"])
        wopool = "PA" if (i2 % 2 == 1) else "PBa"
        wop = h.psum["PA"] if wopool == "PA" else h.psum["PB"]
        kfA = self.tick("f" + wopool)
        self.caps[("a2fpa", i2)] = (wopool, kfA)
        self.caps[("a2wo", i2)] = (wopool, kfA)
        if self.R == "t":
            self.w("ePC", eaot[0])
            self.w("ePD", eaot[1])
            self.pfill(wopool, kfA)
            last = None
            for half in range(2):
                for hcx in range(HC):
                    last = self.e.matmul(
                        wop[:, 512 * half:512 * (half + 1)],
                        h.AOT[:, 128 * hcx:128 * (hcx + 1)],
                        h.woT[:, H * hcx + 512 * half:H * hcx + 512 * (half + 1)],
                        start=(hcx == 0), stop=(hcx == HC - 1))
            self.inc(last, "f" + wopool, 1)
        return (i2, sl, kxr, wopool, kfA)

    def a2_epi(self, ctx):
        h = self.h
        i2, sl, kxr, wopool, kfA = ctx
        st = h.statsB[sl]
        wop = h.psum["PA"] if wopool == "PA" else h.psum["PB"]
        self.tick("e" + wopool)
        ksp = self.tick("sProj")
        ks2 = self.tick("sRstd2")
        if self.R == "s":
            self.w("f" + wopool, kfA)
            if ksp > 1:
                self.w("vX1", ksp - 1)
            s = self.e
            i = s.activation(h.proj[0][:], wop[:], AF.Identity,
                             bias=h.czero[:], accum_out=st[:, 0:1])
            self.inc(i, "e" + wopool, 1)
            s.activation(h.ysb[0][:], h.proj[0][:], AF.Square,
                         accum_out=st[:, 1:2])
            s.activation(st[:, 2:3], st[:, 0:1], AF.Identity,
                         scale=1.0 / H, bias=h.czero[:])
            s.activation(st[:, 3:4], st[:, 1:2], AF.Identity,
                         scale=1.0 / H, bias=h.czero[:])
            s.activation(st[:, 4:5], st[:, 2:3], AF.Square, bias=h.czero[:])
            s.activation(st[:, 5:6], st[:, 4:5], AF.Identity, scale=-1.0,
                         bias=st[:, 3:4])
            s.activation(st[:, 6:7], st[:, 5:6], AF.Ln, bias=h.c_eps[:])
            i = s.activation(st[:, 7:8], st[:, 6:7], AF.Exp, scale=-0.5)
            self.inc(i, "sRstd2", 1)
        kx1 = self.tick("vX1")
        if self.R == "v":
            self.w("sRstd2", ks2)
            self.wdma("dxr", kxr)
            if kx1 > 2:
                self.wdma("dx1w", kx1 - 2)
            self.e.tensor_scalar(h.ysb[0][:], h.proj[0][:], st[:, 2:3],
                                 st[:, 7:8], ALU.subtract, ALU.mult)
            i = self.e.tensor_tensor(h.x1sl[sl][:], h.xr[sl][:], h.ysb[0][:],
                                     ALU.add)
            self.inc(i, "vX1", 1)
        kw = self.tick("dx1w")
        if self.R == "s":
            self.w("vX1", kx1)
            s_, qb = self.caps[("rowof", i2)]
            row = 2048 * s_ + 128 * qb
            d = self.e.dma_start(out=h.x1d[row:row + 128, :],
                                 in_=h.x1sl[sl][:])
            self.idma(d, "dx1w", kw)

    def phase_A(self):
        self.slot_user = {}
        self._units = self.prep_units()
        self._ucur = 0
        groups = [(s_, g) for s_ in range(NSEQ) for g in range(4)]
        self.a1_stage0123(*groups[0])
        pend = None
        for idx, (s_, g) in enumerate(groups):
            self.a1_stage4(s_, g)
            self.k_group(s_, g)
            self.run_prep_units(2)
            nqb = len(A2_BATCH[g])
            subs = {2: [[0, 1], [2, 3]],
                    4: [[0], [1], [2], [3]],
                    6: [[0], [1], [2], [3], [], []]}[nqb]
            for qbi, qb in enumerate(A2_BATCH[g]):
                i2next = self.t["a2"] + 1
                self.caps[("rowof", i2next)] = (s_, qb)
                ctx = self.a2_heads(s_, qb)
                if pend is not None:
                    self.a2_epi(pend)
                pend = ctx
                if idx + 1 < len(groups) and subs[qbi]:
                    self.a1_stage0123(*groups[idx + 1], sub=subs[qbi])
                self.run_prep_units(3)
            if g == 3:
                self.a2_epi(pend)
                pend = None
        self.run_prep_units(10 ** 6)

    def barrier_AB(self):
        caps = [("fPA", self.t["fPA"]), ("ePA", self.t["ePA"]),
                ("fPC", self.t["fPC"]), ("fPD", self.t["fPD"]),
                ("ePC", self.t["ePC"]), ("ePD", self.t["ePD"]),
                ("fPE", self.t["fPE"]), ("fPF", self.t["fPF"]),
                ("ePE", self.t["ePE"]), ("ePF", self.t["ePF"]),
                ("fPBa", self.t["fPBa"]), ("ePBa", self.t["ePBa"]),
                ("fPBb", self.t["fPBb"]), ("ePBb", self.t["ePBb"]),
                ("vX1", self.t["vX1"]), ("vXH", self.t["vXH"]),
                ("sRstd2", self.t["sRstd2"])]
        nw = self.t["dx1w"]
        caps += [("dx1w0", ((nw + 1) // 2) * 16), ("dx1w1", (nw // 2) * 16)]
        nxa = self.t["dxa"]
        for sli in range(4):
            caps += [("dxa%d" % sli, ((nxa - sli + 3) // 4) * 16)]
        nxr = self.t["dxr"]
        caps += [("dxr0", ((nxr + 1) // 2) * 16), ("dxr1", (nxr // 2) * 16)]
        for sem, val in caps:
            self.w(sem, val)

    def phase_B(self):
        h = self.h
        self.barrier_AB()
        self.tick("dwB")
        self.tick("dwB")
        if self.R == "s":
            self.w("dwSt", self.t["dwSt"] * 16)
            d1 = self.e.dma_start(
                out=h.wu_sb[:].rearrange("p (kc i) -> p kc i", i=INNER),
                in_=h.wuTd.rearrange("(kc p) i -> p kc i", p=128))
            self.inc(d1, "dwB", 16)
            d2 = self.e.dma_start(
                out=h.wd_sb[:].rearrange("p (ic o) -> p ic o", o=H),
                in_=h.wdTd.rearrange("(ic p) o -> p ic o", p=128))
            self.inc(d2, "dwB", 16)
        self.b_quant_chain(0)
        self.b_quant_chain_b(0)
        self.b_quant_transpose(0)
        self._tail = None
        for sg in range(NSG):
            hold = {}

            def _head0():
                kvh = self.bdown_head_v(sg, 0)
                hold["k0"] = self.bdown_head_s(sg, 0, kvh)
                self.down_tail()

            for icb in range(ICB - 1):
                self.b_gate_up(sg, icb)
            self.b_gate_up(sg, ICB - 1, headfn=_head0)
            self.caps[("gu_done", sg)] = (self.t["fPF"], self.t["fPBb"])
            kvh1 = self.bdown_head_v(sg, 1)
            if sg + 1 < NSG:
                self.b_quant_chain(sg + 1)

            def _head1():
                hold["k1"] = self.bdown_head_s(sg, 1, kvh1)

            self.bdown_body(sg, 0, SGT * sg, hold["k0"], headfn=_head1)
            if sg + 1 < NSG:
                self.b_quant_chain_b(sg + 1)
            self.bdown_body(sg, 1, SGT * sg + 1, hold["k1"],
                            defer=(sg + 1 < NSG))
            if sg + 1 < NSG:
                self.b_quant_transpose(sg + 1)
        self.tick("fin")
        if self.R == "q":
            n = self.t["dout"]
            self.w("dout0", ((n + 1) // 2) * 16)
            self.w("dout1", (n // 2) * 16)

    def b_quant_chain(self, sg):
        h = self.h
        klds = []
        for i in range(SGT):
            t_ = SGT * sg + i
            self.tick("bq")
            kld = self.tick("dx1r")
            klds.append(kld)
            if self.R == "q":
                nsp = 8 * (t_ // 16 + 1)
                self.w("dx1w0", nsp * 16)
                self.w("dx1w1", nsp * 16)
                if kld > 4:
                    self.w("vOut", kld - 4)
                d = self.e.dma_start(out=h.x1t[(sg % 2) * 2 + i][:],
                                     in_=h.x1d[128 * t_:128 * (t_ + 1), :])
                self.idma(d, "dx1r", kld)
        for i in range(SGT):
            st = h.statsC[i]
            kv2 = self.tick("vBSt2")
            if self.R == "v":
                self.wdma("dx1r", klds[i])
                if kv2 > SGT:
                    self.w("sBRstd", kv2 - SGT)
                v = self.e
                xs = h.x1t[(sg % 2) * 2 + i]
                v.scalar_tensor_tensor(h.junkV[:, 0:1024], xs[:], 1.0,
                                       xs[:], ALU.mult, ALU.mult,
                                       accum_out=st[:, 1:2])
                v.reduce_sum(st[:, 0:1], xs[:], AX.X)
                v.reduce_max(st[:, 2:3], xs[:], AX.X)
                v.tensor_reduce(st[:, 3:4], xs[:], AX.X, ALU.min)
                v.tensor_scalar(st[:, 4:5], st[:, 0:1], 1.0 / H, None,
                                ALU.mult)
                v.tensor_scalar(st[:, 5:6], st[:, 1:2], 1.0 / H, None,
                                ALU.mult)
                v.tensor_tensor(st[:, 6:7], st[:, 4:5], st[:, 4:5], ALU.mult)
                v.tensor_tensor(st[:, 6:7], st[:, 5:6], st[:, 6:7],
                                ALU.subtract)
                v.tensor_tensor(st[:, 7:8], st[:, 2:3], st[:, 4:5],
                                ALU.subtract)
                v.tensor_tensor(st[:, 8:9], st[:, 4:5], st[:, 3:4],
                                ALU.subtract)
                i_ = v.tensor_tensor(st[:, 7:8], st[:, 7:8], st[:, 8:9],
                                     ALU.max)
                self.inc(i_, "vBSt2", 1)
        return klds

    def b_quant_chain_b(self, sg):
        h = self.h
        for i in range(SGT):
            st = h.statsC[i]
            kv2 = self.t["vBSt2"] - SGT + 1 + i
            ks2 = self.tick("sBRstd")
            if self.R == "s":
                self.w("vBSt2", kv2)
                self.e.activation(st[:, 9:10], st[:, 6:7], AF.Ln,
                                  bias=h.c_eps[:])
                i_ = self.e.activation(st[:, 10:11], st[:, 9:10], AF.Exp,
                                       scale=-0.5)
                self.inc(i_, "sBRstd", 1)
        for i in range(SGT):
            st = h.statsC[i]
            ks2 = self.t["sBRstd"] - SGT + 1 + i
            ksu = self.tick("sU")
            self.tick("vBQs")
            if self.R == "v":
                self.w("sBRstd", ks2)
                v = self.e
                v.tensor_tensor(st[:, 11:12], st[:, 7:8], st[:, 10:11],
                                ALU.mult)
                v.tensor_scalar(st[:, 11:12], st[:, 11:12], 1e-5, None,
                                ALU.max)
                v.reciprocal(st[:, 12:13], st[:, 11:12])
                v.tensor_scalar(st[:, 12:13], st[:, 12:13], 127.0, None,
                                ALU.mult)
                v.tensor_tensor(st[:, 13:14], st[:, 10:11], st[:, 12:13],
                                ALU.mult)
                v.scalar_tensor_tensor(st[:, 14:15], st[:, 4:5], -1.0,
                                       st[:, 13:14], ALU.mult, ALU.mult)
                v.tensor_scalar(st[:, 14:15], st[:, 14:15], MAGIC, None,
                                ALU.add)
                v.tensor_scalar(st[:, 15:16], st[:, 11:12], 1.0 / 127.0,
                                None, ALU.mult)
                v.tensor_tensor(st[:, 16:17], st[:, 15:16], h.dqb[0][:],
                                ALU.mult)
                i_ = v.tensor_tensor(st[:, 17:18], st[:, 15:16],
                                     h.dqb[1][:], ALU.mult)
                self.inc(i_, "vBQs", 1)
            til = (ksu + 1) // 2
            if self.R == "v":
                if til > 2:
                    self.wcap(("uq", til - 2))
                usl = (til - 1) % 2
                c = slice(0, 512)
                i_ = self.e.tensor_scalar(h.uact[usl][:, c],
                                          h.x1t[(sg % 2) * 2 + i][:, c],
                                          st[:, 13:14], st[:, 14:15],
                                          ALU.mult, ALU.add)
                self.inc(i_, "sU", 1)
            ksu2 = self.tick("sU")
            if self.R == "v":
                usl = (til - 1) % 2
                c = slice(512, 1024)
                i_ = self.e.tensor_scalar(h.uact[usl][:, c],
                                          h.x1t[(sg % 2) * 2 + i][:, c],
                                          st[:, 13:14], st[:, 14:15],
                                          ALU.mult, ALU.add)
                self.inc(i_, "sU", 1)

    def b_quant_transpose(self, sg):
        h = self.h
        for i in range(SGT):
            ksu_a = self.t["sU"] - 2 * SGT + 1 + 2 * i
            til = (ksu_a + 1) // 2
            for grp in range(2):
                pool = ("PC", "PD", "PE", "PF")[self.tick("btr") % 4]
                self.tick("trgrp")
                kf = self.tick("f" + pool)
                if self.R == "t":
                    self.w("sU", ksu_a + grp)
                    self.pfill(pool, kf)
                    ps = h.psum[pool]
                    last = None
                    for j in range(4):
                        hcj = 4 * grp + j
                        last = self.e.matmul(
                            ps[:, 128 * j:128 * (j + 1)],
                            h.uact[(til - 1) % 2][:, 128 * hcj:128 * (hcj + 1)],
                            h.I16[:], start=True, stop=True)
                    self.inc(last, "f" + pool, 1)
                ke = self.tick("e" + pool)
                if self.R == "v":
                    self.w("f" + pool, kf)
                    if grp == 0 and sg >= 1 and i == 0:
                        gdc = self.caps[("gu_done", sg - 1)]
                        self.w("fPF", gdc[0])
                        self.w("fPBb", gdc[1])
                    i_ = self.e.tensor_scalar(
                        h.xqT[i][:, 512 * grp:512 * (grp + 1)],
                        h.psum[pool][:], -MAGIC, None, ALU.add)
                    self.inc(i_, "e" + pool, 1)
            self.cap(("uq", til))
            self.caps[("xq", SGT * sg + i + 1)] = (self.t["ePC"],
                                                   self.t["ePD"])

    def b_gate_up(self, sg, icb, headfn=None):
        h = self.h
        kwg = self.tick("dwg")
        if self.R == "q":
            self.w("dwSt", 32 * 16)
            if kwg > 2:
                gpp, gpv = self.caps[("gchain", kwg - 2)]
                self.w("f" + gpp, gpv)
            src = h.wgTd[:, 512 * icb:512 * (icb + 1)].rearrange(
                "(kc p) n -> p kc n", p=128)
            d = self.e.dma_start(
                out=h.wgsl[(kwg - 1) % 2][:].rearrange("p (kc n) -> p kc n",
                                                       n=512),
                in_=src)
            self.idma(d, "dwg", kwg)
        for i in range(SGT):
            t_ = SGT * sg + i
            par = self.t["gup"] % 2
            self.tick("gup")
            gp = ("PE", "PBa")[par]
            up = ("PF", "PBb")[par]
            kfg = self.tick("f" + gp)
            kfu = self.tick("f" + up)
            if self.R == "t":
                self.wdma("dwg", kwg)
                self.w("dwB", 32)
                cxq = self.caps[("xq", t_ + 1)]
                self.w("ePC", cxq[0])
                self.w("ePD", cxq[1])
                self.pfill(gp, kfg)
                last = None
                for kc in range(HC):
                    last = self.e.matmul(
                        self.ppsum(gp),
                        h.xqT[i][:, 128 * kc:128 * (kc + 1)],
                        h.wgsl[(kwg - 1) % 2][:, 512 * kc:512 * (kc + 1)],
                        start=(kc == 0), stop=(kc == HC - 1))
                self.inc(last, "f" + gp, 1)
                self.pfill(up, kfu)
                for kc in range(HC):
                    last = self.e.matmul(
                        self.ppsum(up),
                        h.xqT[i][:, 128 * kc:128 * (kc + 1)],
                        h.wu_sb[:, INNER * kc + 512 * icb:
                                INNER * kc + 512 * (icb + 1)],
                        start=(kc == 0), stop=(kc == HC - 1))
                self.inc(last, "f" + up, 1)
            if i == SGT - 1:
                self.caps[("gchain", kwg)] = (gp, kfg)
            keg = self.tick("e" + gp)
            ksg = self.tick("sGs")
            if self.R == "s":
                self.w("f" + gp, kfg)
                prevh = self.caps.get(("hprev", ksg - 1))
                if prevh:
                    self.w("e" + prevh[0], prevh[1])
                st = h.statsC[i]
                i_ = self.e.activation(h.gs_sb[:], self.ppsum(gp), AF.Silu,
                                       scale=st[:, 16:17], bias=h.czero[:])
                self.inc(i_, "e" + gp, 1)
            keu = self.tick("e" + up)
            self.caps[("hprev", ksg)] = (up, keu)
            if self.R == "v":
                self.w("e" + gp, keg)
                self.w("f" + up, kfu)
                if sg > 0 and icb == 0:
                    self.w("sUh", SGT * (sg - 1) + i + 1)
                st = h.statsC[i]
                self.e.scalar_tensor_tensor(
                    h.Hb[i][:, 512 * icb:512 * (icb + 1)], h.gs_sb[:],
                    st[:, 17:18], self.ppsum(up), ALU.mult, ALU.mult)
                i_ = self.e.tensor_reduce(
                    h.hpart[i][:, icb:icb + 1],
                    h.Hb[i][:, 512 * icb:512 * (icb + 1)], AX.X, ALU.max,
                    apply_absolute_value=True)
                self.inc(i_, "e" + up, 1)
            if i == 0 and headfn is not None:
                headfn()

    def emit_down(self, rec, icg, kfD, pooldown):
        h = self.h
        pool, ke, kvq = rec
        if self.R == "t":
            self.w("e" + pool, ke)
            if icg == 0:
                self.pfill(pooldown, kfD)
            sl2 = (kvq - 1) % 4
            last = None
            for j in range(4):
                icj = 4 * icg + j
                for half in range(2):
                    last = self.e.matmul(
                        h.psum[pooldown][:, 512 * half:512 * (half + 1)],
                        h.hqT[sl2][:, 128 * j:128 * (j + 1)],
                        h.wd_sb[:, 1024 * icj + 512 * half:
                                1024 * icj + 512 * (half + 1)],
                        start=(icj == 0), stop=(icj == IC - 1))
            if icg == 7:
                self.inc(last, "f" + pooldown, 1)

    def bdown_head_v(self, sg, i):
        h = self.h
        st = h.statsC[i]
        o = 0
        kvh = self.tick("vHmax")
        if self.R == "v":
            v = self.e
            v.tensor_reduce(st[:, 18+o:19+o], h.hpart[i][:], AX.X, ALU.max)
            v.tensor_scalar(st[:, 18+o:19+o], st[:, 18+o:19+o], 1e-5, None,
                            ALU.max)
            v.reciprocal(st[:, 19+o:20+o], st[:, 18+o:19+o])
            v.tensor_scalar(st[:, 19+o:20+o], st[:, 19+o:20+o], 127.0, None,
                            ALU.mult)
            v.tensor_scalar(st[:, 20+o:21+o], st[:, 18+o:19+o], 1.0 / 127.0,
                            None, ALU.mult)
            i_ = v.tensor_tensor(st[:, 21+o:22+o], st[:, 20+o:21+o],
                                 h.dqb[2][:], ALU.mult)
            self.inc(i_, "vHmax", 1)
        return kvh

    def bdown_head_s(self, sg, i, kvh):
        h = self.h
        st = h.statsC[i]
        last = None
        for q in range(2):
            ksq = self.tick("sUh")
            if self.R == "s":
                if q == 0:
                    self.w("vHmax", kvh)
                    if ksq > 2:
                        self.wcap(("uhq", (ksq - 1) // 2))
                c = slice(2048 * q, 2048 * (q + 1))
                i_ = self.e.activation(h.uh[:, c], h.Hb[i][:, c],
                                       AF.Identity, scale=st[:, 19:20],
                                       bias=h.b1536[:])
                self.inc(i_, "sUh", 1)
            last = ksq
        return last

    def bdown_body(self, sg, i, t_, ksu, headfn=None, defer=False):
        h = self.h
        st = h.statsC[i]
        x1slot = (sg % 2) * 2 + i
        pooldown = "PA"
        kfD = self.tick("f" + pooldown)
        trrec = []
        for icg in range(8):
            pool = ("PC", "PD", "PE", "PF")[self.tick("btr") % 4]
            self.tick("trgrp")
            kf = self.tick("f" + pool)
            if self.R == "t":
                self.w("sUh", ksu - 1 if icg < 4 else ksu)
                self.pfill(pool, kf)
                ps = h.psum[pool]
                last = None
                for j in range(4):
                    icj = 4 * icg + j
                    last = self.e.matmul(ps[:, 128 * j:128 * (j + 1)],
                                         h.uh[:, 128 * icj:128 * (icj + 1)],
                                         h.I16[:], start=True, stop=True)
                self.inc(last, "f" + pool, 1)
            ke = self.tick("e" + pool)
            kvq = self.tick("vHq")
            if self.R == "s":
                self.w("f" + pool, kf)
                i_ = self.e.activation(h.hqT[(kvq - 1) % 4][:],
                                       h.psum[pool][:], AF.Identity,
                                       bias=h.cm1536[:])
                self.inc(i_, "e" + pool, 1)
            trrec.append((pool, ke, kvq))
            if icg >= 3:
                self.emit_down(trrec[icg - 3], icg - 3, kfD, pooldown)
        self.cap(("uhq", ksu // 2))
        if headfn is not None:
            headfn()
        if defer:
            self._tail = (trrec, kfD, sg, i, t_)
            return
        self.down_tail_emit(trrec, kfD, sg, i, t_)

    TAIL0 = 5

    def down_tail(self):
        if self._tail is None:
            return
        trrec, kfD, sg, i, t_ = self._tail
        self._tail = None
        self.down_tail_emit(trrec, kfD, sg, i, t_)

    def down_tail_emit(self, trrec, kfD, sg, i, t_):
        h = self.h
        st = h.statsC[i]
        x1slot = (sg % 2) * 2 + i
        pooldown = "PA"
        for icg in range(self.TAIL0, 8):
            self.emit_down(trrec[icg], icg, kfD, pooldown)
        keD = self.tick("e" + pooldown)
        ksd = self.tick("sD")
        dsl = (ksd - 1) % 2
        if self.R == "s":
            self.w("f" + pooldown, kfD)
            if ksd > 2:
                self.w("vOut", ksd - 2)
            i_ = self.e.activation(h.d_sb[dsl][:], h.psum[pooldown][:],
                                   AF.Identity, scale=st[:, 21:22],
                                   bias=h.czero[:], accum_out=st[:, 22:23])
            self.inc(i_, "e" + pooldown, 1)
        kvs = self.tick("vDSq")
        if self.R == "v":
            self.w("ePA", keD)
            v = self.e
            v.scalar_tensor_tensor(h.junkV[:], h.d_sb[dsl][:], 1.0,
                                   h.d_sb[dsl][:], ALU.mult, ALU.mult,
                                   accum_out=st[:, 23:24])
            v.tensor_scalar(st[:, 24:25], st[:, 22:23], 1.0 / H, None,
                            ALU.mult)
            v.tensor_scalar(st[:, 25:26], st[:, 23:24], 1.0 / H, None,
                            ALU.mult)
            v.tensor_tensor(st[:, 26:27], st[:, 24:25], st[:, 24:25],
                            ALU.mult)
            i_ = v.tensor_tensor(st[:, 26:27], st[:, 25:26], st[:, 26:27],
                                 ALU.subtract)
            self.inc(i_, "vDSq", 1)
        ks3 = self.tick("sDR")
        if self.R == "s":
            self.w("vDSq", kvs)
            s = self.e
            s.activation(st[:, 27:28], st[:, 26:27], AF.Ln, bias=h.c_eps[:])
            i_ = s.activation(st[:, 28:29], st[:, 27:28], AF.Exp, scale=-0.5)
            self.inc(i_, "sDR", 1)
        kout = self.tick("vOut")
        if self.R == "v":
            self.w("sDR", ks3)
            if kout > 2:
                self.wdma("dout", kout - 2)
            sl2 = (kout - 1) % 2
            self.e.tensor_scalar(h.d_sb[dsl][:], h.d_sb[dsl][:],
                                 st[:, 24:25], st[:, 28:29], ALU.subtract,
                                 ALU.mult)
            i_ = self.e.tensor_tensor(h.out_sb[sl2][:], h.x1t[x1slot][:],
                                      h.d_sb[dsl][:], ALU.add)
            self.inc(i_, "vOut", 1)
        kst = self.tick("dout")
        if self.R == "s":
            self.w("vOut", kout)
            sl2 = (kout - 1) % 2
            d = self.e.dma_start(out=h.yflat[128 * t_:128 * (t_ + 1), :],
                                 in_=h.out_sb[sl2][:])
            self.idma(d, "dout", kst)

    def emit_down(self, rec, icg, kfD, pooldown):
        h = self.h
        pool, ke, kvq = rec
        if self.R == "t":
            self.w("e" + pool, ke)
            if icg == 0:
                self.pfill(pooldown, kfD)
            sl2 = (kvq - 1) % 4
            last = None
            for j in range(4):
                icj = 4 * icg + j
                for half in range(2):
                    last = self.e.matmul(
                        h.psum[pooldown][:, 512 * half:512 * (half + 1)],
                        h.hqT[sl2][:, 128 * j:128 * (j + 1)],
                        h.wd_sb[:, 1024 * icj + 512 * half:
                                1024 * icj + 512 * (half + 1)],
                        start=(icj == 0), stop=(icj == IC - 1))
            if icg == 7:
                self.inc(last, "f" + pooldown, 1)

class Handles:
    pass


def build_and_emit():
    nc = bass.Bass("TRN2", target_bir_lowering=False)
    h = Handles()
    h.x = nc.dram_tensor("x", [NSEQ, S, H], F32, kind="ExternalInput").ap()
    h.wq = nc.dram_tensor("wq", [H, H], F32, kind="ExternalInput").ap()
    h.wk = nc.dram_tensor("wk", [H, H], F32, kind="ExternalInput").ap()
    h.wv = nc.dram_tensor("wv", [H, H], F32, kind="ExternalInput").ap()
    h.wo = nc.dram_tensor("wo", [H, H], F32, kind="ExternalInput").ap()
    h.wg = nc.dram_tensor("wg", [INNER, H], F32, kind="ExternalInput").ap()
    h.wu = nc.dram_tensor("wu", [INNER, H], F32, kind="ExternalInput").ap()
    h.wd = nc.dram_tensor("wd", [H, INNER], F32, kind="ExternalInput").ap()
    h.y = nc.dram_tensor("y", [NSEQ, S, H], F32, kind="ExternalOutput").ap()
    h.yflat = h.y.rearrange("s t h -> (s t) h")
    h.x1d = nc.dram_tensor("x1d", [NSEQ * S, H], BF16, kind="Internal").ap()
    h.wgTd = nc.dram_tensor("wgTd", [H, INNER], BF16, kind="Internal").ap()
    h.wuTd = nc.dram_tensor("wuTd", [H, INNER], BF16, kind="Internal").ap()
    h.wdTd = nc.dram_tensor("wdTd", [INNER, H], BF16, kind="Internal").ap()

    es0 = ExitStack()
    sb = lambda name, shape, dt: es0.enter_context(
        nc.sbuf_tensor(name, shape, dt))
    h.Ib = sb("Ib", [128, 128], BF16)
    h.nIb = sb("nIb", [128, 128], BF16)
    h.I16 = sb("I16", [128, 128], FP16)
    h.m1536row = sb("m1536row", [1, 128], FP16)
    h.ones16row = sb("ones16row", [1, 128], FP16)
    h.onesf32 = sb("onesf32", [128, 1], F32)
    h.onesf32row = sb("onesf32row", [1, 128], F32)
    h.b1536 = sb("b1536", [128, 1], F32)
    h.cm12 = sb("cm12", [128, 1], F32)
    h.cm1536 = sb("cm1536", [128, 1], F32)
    h.c_eps = sb("c_eps", [128, 1], F32)
    h.czero = sb("czero", [128, 1], F32)
    h.junkV = sb("junkV", [128, 1024], BF16)
    h.statsA = [sb(f"statsA{i}", [128, 8], F32) for i in range(4)]
    h.statsB = [sb(f"statsB{i}", [128, 12], F32) for i in range(2)]
    h.statsC = [sb(f"statsC{i}", [128, 32], F32) for i in range(SGT)]
    h.sabs = sb("sabs", [128, 1], F32)
    h.babs = sb("babs", [128, 1], F32)
    h.wsc1 = sb("wsc1", [1, 1], F32)
    h.wsc2 = sb("wsc2", [1, 1], F32)
    h.wsb = [sb(f"wsb{i}", [128, 1], F32) for i in range(3)]
    h.dqb = [sb(f"dqb{i}", [128, 1], F32) for i in range(3)]

    esp = ExitStack()
    h.psum = {}
    h.psum["PA"] = esp.enter_context(nc.psum_tensor("PA", [128, 1024], F32))
    h.psum["PB"] = esp.enter_context(nc.psum_tensor("PB", [128, 1024], F32))
    for p in ("PC", "PD", "PE", "PF"):
        h.psum[p] = esp.enter_context(nc.psum_tensor(p, [128, 512], F32))

    ess = ExitStack()
    h.sems = {}
    semnames = ["gc", "gser",
                "dwA0", "dwA1", "dwP0", "dwP1", "dxa0", "dxa1", "dxa2", "dxa3", "vXH",
                "dxr0", "dxr1", "dx1w0", "dx1w1", "dx1r0", "dx1r1",
                "dwg0", "dwg1", "dout0", "dout1", "dwSt", "dwB",
                "vWC", "fWP", "vZabs", "vScale", "vBc",
                "vSt1", "sSq", "vSt2", "sRstd", "vBias", "sXH",
                "vMask", "sProj", "vPSq", "sRstd2", "vBias2", "sY", "vX1",
                "vBSt1", "sBSq", "vBSt2", "sBRstd", "vBQs", "sU", "vXq",
                "sGs", "vHmax", "sUh", "sD", "vDSq", "sDR", "vDB", "sYd",
                "vOut", "vser", "sser", "tAVa", "gp1",
                "fPA", "ePA", "fPBa", "ePBa", "fPBb", "ePBb", "fPC", "ePC", "fPD", "ePD",
                "fPE", "ePE", "fPF", "ePF"]
    for sname in semnames:
        h.sems[sname] = ess.enter_context(nc.semaphore(sname))

    esA = ExitStack()
    sbA = lambda name, shape, dt: esA.enter_context(
        nc.sbuf_tensor(name, shape, dt))
    h.maskbuf = {}
    for mname in ("QB0A", "C0x4", "QB1", "MID", "QB15"):
        h.maskbuf[mname] = sbA("M_" + mname, [128, 512], BF16)
    h.M_QB0A = h.maskbuf["QB0A"]; h.M_C0x4 = h.maskbuf["C0x4"]
    h.M_QB1 = h.maskbuf["QB1"]; h.M_MID = h.maskbuf["MID"]
    h.M_QB15 = h.maskbuf["QB15"]
    h.wqT = sbA("wqT", [128, HC * H], BF16)
    h.wkT = sbA("wkT", [128, HC * H], BF16)
    h.wvT = sbA("wvT", [128, HC * H], BF16)
    h.woT = sbA("woT", [128, HC * H], BF16)
    h.wld = [sbA(f"wld{i}", [128, 1024], F32) for i in range(2)]
    h.wcast = [sbA("wcast0", [128, 1024], BF16)]
    h.c1s = [sbA(f"c1s{i}", [128, 1024], BF16) for i in range(2)]
    h.c2s = [sbA(f"c2s{i}", [128, 1024], BF16) for i in range(2)]
    h.wtst = [sbA("wtst0", [128, 1024], BF16)]
    h.QTr = sbA("QTr", [128, HC * 6 * 128], BF16)
    h.KT = sbA("KT", [128, HC * S], BF16)
    h.Vt = sbA("Vt", [128, NT * (NH * 65)], BF16)
    h.xt = [sbA(f"xt{i}", [128, 1024], BF16) for i in range(4)]
    h.xhT4 = sbA("xhT4", [128, HC * 512], BF16)
    h.xr = [sbA(f"xr{i}", [128, 1024], BF16) for i in range(2)]
    h.x1sl = [sbA(f"x1sl{i}", [128, 1024], BF16) for i in range(2)]
    h.Asb = sbA("Asb", [128, 2048], BF16)
    h.AO = sbA("AO", [128, 1024], BF16)
    h.AOT = sbA("AOT", [128, 1024], BF16)
    h.proj = [sbA("proj0", [128, 1024], BF16)]
    h.ysb = [sbA("ysb0", [128, 1024], BF16)]

    counts = {}
    capsd = {}
    sercnts = {}
    with nc.Block() as block:
        def emit_role(role, eng):
            wlk = Walk(h, role, eng)
            wlk.phase_consts()
            wlk.phase_prep()
            wlk.phase_A()
            capsd[role] = wlk.caps
            if isinstance(wlk.e, ChainEng):
                sercnts[role] = wlk.e.cnt
            return dict(wlk.t)

        @block.tensor
        def _(t):
            counts["t"] = emit_role("t", t)

        @block.vector
        def _(v):
            counts["v"] = emit_role("v", v)

        @block.scalar
        def _(s):
            counts["s"] = emit_role("s", s)

        @block.sync
        def _(q):
            counts["q"] = emit_role("q", q)

        @block.gpsimd
        def _(g):
            counts["g"] = emit_role("g", g)
    base = counts["t"]
    for r in "vsqg":
        assert counts[r] == base, f"tick divergence (A) role {r}"
        assert capsd[r] == capsd["t"], f"caps divergence (A) role {r}"

    esA.close()
    esB = ExitStack()
    sbB = lambda name, shape, dt: esB.enter_context(
        nc.sbuf_tensor(name, shape, dt))
    h.wu_sb = sbB("wu_sb", [128, HC * INNER], BF16)
    h.wd_sb = sbB("wd_sb", [128, IC * H], BF16)
    h.wgsl = [sbB(f"wgsl{i}", [128, HC * 512], BF16) for i in range(2)]
    h.Hb = [sbB(f"Hb{i}", [128, INNER], BF16) for i in range(SGT)]
    h.xqT = [sbB(f"xqT{i}", [128, 1024], BF16) for i in range(SGT)]
    h.x1t = [sbB(f"x1t{i}", [128, 1024], BF16) for i in range(4)]
    h.uact = [sbB(f"uact{i}", [128, 1024], FP16) for i in range(2)]
    h.uh = sbB("uh", [128, INNER], FP16)
    h.hqT = [sbB(f"hqT{i}", [128, 512], BF16) for i in range(4)]
    h.gs_sb = sbB("gs_sb", [128, 512], BF16)
    h.d_sb = [sbB(f"d_sb{i}", [128, 1024], BF16) for i in range(2)]
    
    h.out_sb = [sbB(f"out_sb{i}", [128, 1024], F32) for i in range(2)]
    h.hpart = [sbB(f"hpart{i}", [128, 8], F32) for i in range(SGT)]

    counts2 = {}
    caps2 = {}
    with nc.Block() as block:
        def emit_roleB(role, eng):
            wlk = Walk(h, role, eng, counts[role],
                       sercnt=sercnts.get(role, 0))
            wlk.caps = dict(capsd[role])
            wlk.phase_B()
            caps2[role] = wlk.caps
            return dict(wlk.t)

        @block.tensor
        def _(t):
            counts2["t"] = emit_roleB("t", t)

        @block.vector
        def _(v):
            counts2["v"] = emit_roleB("v", v)

        @block.scalar
        def _(s):
            counts2["s"] = emit_roleB("s", s)

        @block.sync
        def _(q):
            counts2["q"] = emit_roleB("q", q)

        @block.gpsimd
        def _(g):
            counts2["g"] = emit_roleB("g", g)
    base = counts2["t"]
    for r in "vsqg":
        assert counts2[r] == base, f"tick divergence (B) role {r}"
        assert caps2[r] == caps2["t"], f"caps divergence (B) role {r}"
    esB.close()
    ess.close()
    esp.close()
    es0.close()
    return nc


_PROGRAM_CACHE = {}


def _get_program():
    if "nc" not in _PROGRAM_CACHE:
        _PROGRAM_CACHE["nc"] = build_and_emit()
    return _PROGRAM_CACHE["nc"]


def kernel(L0_attn_x, L0_attn_Wq, L0_attn_Wk, L0_attn_Wv, L0_attn_Wo,
           L0_ffn_w_gate, L0_ffn_w_up, L0_ffn_w_down):
    nc = _get_program()
    x = np.ascontiguousarray(np.asarray(L0_attn_x, dtype=np.float32))
    args = {
        "wq": np.ascontiguousarray(np.asarray(L0_attn_Wq, dtype=np.float32)),
        "wk": np.ascontiguousarray(np.asarray(L0_attn_Wk, dtype=np.float32)),
        "wv": np.ascontiguousarray(np.asarray(L0_attn_Wv, dtype=np.float32)),
        "wo": np.ascontiguousarray(np.asarray(L0_attn_Wo, dtype=np.float32)),
        "wg": np.ascontiguousarray(np.asarray(L0_ffn_w_gate, dtype=np.float32)),
        "wu": np.ascontiguousarray(np.asarray(L0_ffn_w_up, dtype=np.float32)),
        "wd": np.ascontiguousarray(np.asarray(L0_ffn_w_down, dtype=np.float32)),
    }
    in_maps = []
    for c in range(NCORES):
        m = {"x": np.ascontiguousarray(x[NSEQ * c:NSEQ * (c + 1)])}
        m.update(args)
        in_maps.append(m)
    res = run_bass_kernel_spmd(nc, in_maps, core_ids=list(range(NCORES)))
    return np.concatenate([r["y"] for r in res.results], axis=0)
